# version 06
# baseline (speedup 1.0000x reference)
import numpy as np
import ml_dtypes
from contextlib import ExitStack
import concourse.bass as bass
import concourse.mybir as mybir
from concourse.bass_utils import run_bass_kernel_spmd

F32 = mybir.dt.float32
BF16 = mybir.dt.bfloat16
I8 = mybir.dt.int8
AF = mybir.ActivationFunctionType
ALU = mybir.AluOpType
BFNP = ml_dtypes.bfloat16

T = 2048
D = 4096
J = 18560
IN = 8192
CV = 10240
NH = 128
HD = 64
NG = 8
SS = 128
L = 128
NCH = 16
NJT = J // 128
EPS = 1e-5
DT_MIN, DT_MAX = 1e-3, 100.0
NEG = -1e30

ENGINES = ("pe", "dve", "act", "sp", "pool")
_BLOCK_ATTR = {"pe": "tensor", "dve": "vector", "act": "scalar", "sp": "sync", "pool": "gpsimd"}
DMA_SEM = {"sp": "dsp", "act": "dact", "pool": "dpool"}
TOTAL_SEMS = ("dc0", "dc5", "dc6")
XTR_SEMS = ("dtr", "dtr2", "dspl")
ALL_SEMS = list(ENGINES) + list(DMA_SEM.values()) + list(TOTAL_SEMS) + list(XTR_SEMS)


class _Op:
    __slots__ = ("fn", "sem", "dma", "marked", "waits", "val")

    def __init__(self, fn, sem, dma, waits):
        self.fn = fn
        self.sem = sem
        self.dma = dma
        self.marked = dma
        self.waits = waits
        self.val = None


class Prog:

    def __init__(self):
        self.ops = {e: [] for e in ENGINES}
        self.order = []
        self.last_dma = {}

    def op(self, eng, fn, waits=(), inc=None):
        ws = [w for w in waits if w is not None]
        o = _Op(fn, eng, False, ws)
        for w in ws:
            if isinstance(w, _Op):
                w.marked = True
        self.ops[eng].append(o)
        self.order.append((eng, o))
        return o

    def dma(self, q, out, in_, waits=(), transpose=False, sem=None):
        ws = [w for w in waits if w is not None]
        if transpose:
            fn = lambda e, out=out, in_=in_: e.dma_start_transpose(out, in_)
        else:
            fn = lambda e, out=out, in_=in_: e.dma_start(out=out, in_=in_)
        o = _Op(fn, sem or DMA_SEM[q], True, ws)
        for w in ws:
            if isinstance(w, _Op):
                w.marked = True
        self.ops[q].append(o)
        self.order.append((q, o))
        self.last_dma[o.sem] = o
        return o

    def last(self, eng):
        for o in reversed(self.ops[eng]):
            if o.fn is not None:
                return o
        return None

    def barrier(self):
        lasts = [self.last(e) for e in ENGINES] + list(self.last_dma.values())
        lasts = [o for o in lasts if o is not None]
        for o in lasts:
            o.marked = True
        for e in ENGINES:
            self.op(e, None, waits=list(lasts))

    def finalize(self):
        cnt = {s: 0 for s in ALL_SEMS}
        for (eng, o) in self.order:
            if o.marked:
                assert o.fn is not None or o.dma, "marked no-op would never increment"
                cnt[o.sem] += 16 if o.dma else 1
                o.val = cnt[o.sem]
        self.total = cnt

    def emit(self, nc, block, sems):
        self.finalize()
        for e in ENGINES:
            attr = _BLOCK_ATTR[e]
            ops = self.ops[e]

            def section(h, ops=ops, e=e):
                hwm = {s: 0 for s in ALL_SEMS}
                for o in ops:
                    for w in o.waits:
                        assert isinstance(w, _Op) and w.marked and w.val is not None
                        s = w.sem
                        v = self.total[s] if s in TOTAL_SEMS else w.val
                        if hwm[s] < v:
                            hwm[s] = v
                            h.wait_ge(sems[s], v)
                    if o.fn is not None:
                        ins = o.fn(h)
                        if o.marked and not o.dma:
                            ins.then_inc(sems[o.sem], 1)
                        elif o.dma:
                            ins.then_inc(sems[o.sem], 16)

            getattr(block, attr)(section)


class Arena:

    def __init__(self, ap_int8):
        self.base = ap_int8
        self.nbytes = ap_int8.shape[1]
        self.off = 0
        self.high = self.nbytes
        self.peak = 0

    def alloc(self, free_elems, dtype, align=64):
        esz = {F32: 4, BF16: 2, I8: 1}[dtype]
        self.off = (self.off + align - 1) // align * align
        nb = free_elems * esz
        ap = self.base[:, self.off:self.off + nb]
        self.off += nb
        self.peak = max(self.peak, self.off)
        assert self.off <= self.high, f"arena overflow {self.off} > {self.high}"
        if dtype == I8:
            return ap
        return ap.bitcast(dtype)

    def alloc_high(self, free_elems, dtype, align=64):
        esz = {F32: 4, BF16: 2, I8: 1}[dtype]
        nb = (free_elems * esz + align - 1) // align * align
        self.high -= nb
        assert self.off <= self.high, "arena overflow (high)"
        ap = self.base[:, self.high:self.high + free_elems * esz]
        return ap if dtype == I8 else ap.bitcast(dtype)

    def alloc2(self, p, f, dtype, align=512):
        ap = self.alloc(f, dtype, align)
        return ap.rearrange("p (a f) -> p a f", a=1)[:, 0, :] if False else ap

    def mark(self):
        return self.off

    def reset(self, mark):
        self.off = mark


def _r2(ap, *shape):
    s = "p (" + " ".join(f"a{i}" for i in range(len(shape))) + ") -> p " + " ".join(f"a{i}" for i in range(len(shape)))
    return ap.rearrange(s, **{f"a{i}": n for i, n in enumerate(shape)})


def build(upto="full", debug=False):
    nc = bass.Bass("TRN2", target_bir_lowering=False)
    kind_spill = "ExternalOutput" if debug else "Internal"

    x_h = nc.dram_tensor("x", [T, D], F32, kind="ExternalInput").ap()
    win_h = nc.dram_tensor("win", [NJT, 128, 32, 128], F32, kind="ExternalInput").ap()
    cw_h = nc.dram_tensor("cw", [128, 80, 4], F32, kind="ExternalInput").ap()
    cb_h = nc.dram_tensor("cb", [128, 80], F32, kind="ExternalInput").ap()
    dtb_h = nc.dram_tensor("dtb", [128, 1], F32, kind="ExternalInput").ap()
    dvec_h = nc.dram_tensor("dvec", [1, 128], F32, kind="ExternalInput").ap()
    nw_h = nc.dram_tensor("nw", [128, 64], F32, kind="ExternalInput").ap()
    wout_h = nc.dram_tensor("wout", [IN, D], F32, kind="ExternalInput").ap()
    wk_h = nc.dram_tensor("wk", [D, D], F32, kind="ExternalInput").ap()
    fgate_h = nc.dram_tensor("fgate", [1, 16], F32, kind="ExternalInput").ap()
    fbias_h = nc.dram_tensor("fbias", [1, D], F32, kind="ExternalInput").ap()
    lA_h = nc.dram_tensor("lA", [16, 128, 1024], F32, kind="ExternalInput").ap()
    lB_h = nc.dram_tensor("lB", [16, 32, D], F32, kind="ExternalInput").ap()
    lAs_h = nc.dram_tensor("lAs", [128, 1024], F32, kind="ExternalInput").ap()
    lBs_h = nc.dram_tensor("lBs", [32, D], F32, kind="ExternalInput").ap()
    idf_h = nc.dram_tensor("idf", [128, 128], F32, kind="ExternalInput").ap()
    idb_h = nc.dram_tensor("idb", [128, 128], BF16, kind="ExternalInput").ap()
    mt_h = nc.dram_tensor("mt", [128, 128], F32, kind="ExternalInput").ap()
    acol_h = nc.dram_tensor("acol", [128, 1], F32, kind="ExternalInput").ap()
    ones_h = nc.dram_tensor("onesf", [1, 128], F32, kind="ExternalInput").ap()

    xbc_sp = nc.dram_tensor("xbc_sp", [CV, T], BF16, kind=kind_spill).ap()
    gate_sp = nc.dram_tensor("gate_sp", [IN, T], BF16, kind=kind_spill).ap()
    nrm_sp = nc.dram_tensor("nrm_sp", [T, IN], BF16, kind=kind_spill).ap()
    x1_sp = nc.dram_tensor("x1_sp", [T, D], F32, kind=kind_spill).ap()
    woutnw_sp = nc.dram_tensor("woutnw_sp", [IN, D], F32, kind="Internal").ap()
    y_h = nc.dram_tensor("y", [T, D], F32, kind="ExternalOutput").ap()
    dbg = {}
    if debug:
        dbg["dtv"] = nc.dram_tensor("dbg_dtv", [128, T], F32, kind="ExternalOutput").ap()
        dbg["cs"] = nc.dram_tensor("dbg_cs", [128, T], F32, kind="ExternalOutput").ap()
        dbg["xnt"] = nc.dram_tensor("dbg_xnt", [128, 32 * 2048], BF16, kind="ExternalOutput").ap()
        dbg["stat"] = nc.dram_tensor("dbg_stat", [128, 64], F32, kind="ExternalOutput").ap()
        dbg["y15"] = nc.dram_tensor("dbg_y15", [128, IN], BF16, kind="ExternalOutput").ap()
        dbg["scb15"] = nc.dram_tensor("dbg_scb15", [128, 1024], BF16, kind="ExternalOutput").ap()
        dbg["xdt15"] = nc.dram_tensor("dbg_xdt15", [128, 512], BF16, kind="ExternalOutput").ap()
        dbg["prevb15"] = nc.dram_tensor("dbg_prevb15", [128, IN], BF16, kind="ExternalOutput").ap()
        dbg["p15"] = nc.dram_tensor("dbg_p15", [128, 1024], F32, kind="ExternalOutput").ap()
        dbg["r16"] = nc.dram_tensor("dbg_r16", [128, 32], F32, kind="ExternalOutput").ap()
        dbg["axs"] = nc.dram_tensor("dbg_axs", [128, 2048], BF16, kind="ExternalOutput").ap()
        dbg["bs"] = nc.dram_tensor("dbg_bs", [128, 4096], BF16, kind="ExternalOutput").ap()
        dbg["xn2t"] = nc.dram_tensor("dbg_xn2t", [128, 32 * 2048], BF16, kind="ExternalOutput").ap()
        dbg["s1p"] = nc.dram_tensor("dbg_s1p", [128, 256], F32, kind="ExternalOutput").ap()

    prog = Prog()
    with ExitStack() as ctx:
        E = ctx.enter_context
        sems = {s: E(nc.semaphore(s)) for s in ALL_SEMS}
        sb = E(nc.sbuf_tensor("arena", [128, 200 * 1024], I8))
        psA = E(nc.psum_tensor("psA", [128, 2048], F32))
        psB = E(nc.psum_tensor("psB", [128, 2048], F32))
        ar = Arena(sb[:, :])

        idf = ar.alloc(128, F32)
        idb = ar.alloc(128, BF16)
        mt = ar.alloc(128, F32)
        acol = ar.alloc(1, F32)
        dtbc = ar.alloc(1, F32)
        onesf = ar.alloc(128, F32)
        dvrow = ar.alloc(128, F32)
        dbc = ar.alloc(128, F32)
        cwt = ar.alloc(80 * 4, F32)
        cbt = ar.alloc(80, F32)
        dtv = ar.alloc_high(T, F32)
        cs = ar.alloc_high(T, F32)
        zer128 = ar.alloc(128, F32)
        cst = ar.alloc(4, F32)

        m_persist = ar.mark()

        prog.dma("act", idf, idf_h, sem="dc0")
        prog.dma("act", idb, idb_h, sem="dc0")
        prog.dma("act", mt, mt_h, sem="dc0")
        prog.dma("act", acol, acol_h, sem="dc0")
        prog.dma("act", dtbc, dtb_h, sem="dc0")
        prog.dma("act", onesf[0:1, :], ones_h, sem="dc0")
        prog.dma("act", dvrow[0:1, :], dvec_h, sem="dc0")
        prog.dma("act", _r2(cwt, 80, 4), cw_h, sem="dc0")
        nwc = ar.alloc(64, F32)
        prog.dma("act", cbt, cb_h, sem="dc0")
        c0 = prog.dma("act", nwc, nw_h, sem="dc0")
        prog.op("dve", lambda v: v.memset(zer128, 0.0))
        prog.op("dve", lambda v: v.memset(cst[:, 0:1], EPS))
        prog.op("dve", lambda v: v.memset(cst[:, 1:2], 1.0))
        pb = psA[:, 0:128]
        m = prog.op("pe", lambda t: t.matmul(pb, onesf[0:1, :], dvrow[0:1, :], start=True, stop=True),
                    waits=[c0])
        mil_dbc = prog.op("dve", lambda v: v.tensor_copy(dbc, pb), waits=[m])

        m_p1 = ar.mark()
        xnT = ar.alloc(32 * 2048, BF16)
        xnT3 = _r2(xnT, 32, 2048)
        if debug:
            dbgstat = ar.alloc(64, F32)
        m_p1a = ar.mark()
        xt = [ar.alloc(D, F32), ar.alloc(D, F32)]
        xnb = ar.alloc(D, BF16)
        sqd = ar.alloc(2048, F32)
        s1 = ar.alloc(1, F32)
        s2a = ar.alloc(1, F32)
        s2b = ar.alloc(1, F32)
        meanc = ar.alloc(1, F32)
        msq = ar.alloc(1, F32)
        varc = ar.alloc(1, F32)
        rstd = ar.alloc(1, F32)
        nb = ar.alloc(1, F32)

        xload = [None, None]
        xn_done = [None, None]
        tr_done = None
        for tt in range(16):
            par = tt % 2
            xload[par] = prog.dma("sp", xt[par], x_h[tt * 128:(tt + 1) * 128, :],
                                  waits=[xn_done[par]] if xn_done[par] else [])
            wl = xload[par]
            m1 = prog.op("dve", lambda v, p=par: v.tensor_reduce(s1, xt[p], mybir.AxisListType.X, ALU.add),
                         waits=[wl], inc="dve")
            m2 = prog.op("act", lambda a, p=par: a.activation(sqd, xt[p][:, 0:2048], AF.Square, accum_out=s2a),
                         waits=[wl], inc="act")
            m3 = prog.op("act", lambda a, p=par: a.activation(sqd, xt[p][:, 2048:4096], AF.Square, accum_out=s2b),
                         inc="act")
            m4 = prog.op("dve", lambda v: v.tensor_scalar(meanc, s1, 1.0 / D, None, ALU.mult), waits=[m1], inc="dve")
            m5 = prog.op("dve", lambda v: v.tensor_tensor(msq, meanc, meanc, ALU.mult), waits=[m4], inc="dve")
            m6 = prog.op("dve", lambda v: v.tensor_tensor(varc, s2a, s2b, ALU.add), waits=[m3], inc="dve")
            m7 = prog.op("dve", lambda v: v.scalar_tensor_tensor(rstd, varc, 1.0 / D, msq, ALU.mult, ALU.subtract),
                         waits=[m5, m6], inc="dve")
            m8 = prog.op("act", lambda a: a.activation(rstd, rstd, AF.Ln, bias=cst[:, 0:1]), waits=[m7], inc="act")
            m9 = prog.op("act", lambda a: a.activation(rstd, rstd, AF.Exp, scale=-0.5), inc="act")
            m10 = prog.op("dve", lambda v: v.scalar_tensor_tensor(nb, meanc, -1.0, rstd, ALU.mult, ALU.mult),
                          waits=[m9], inc="dve")
            if debug:
                prog.op("dve", lambda v, tt=tt: v.tensor_copy(dbgstat[:, tt * 4 + 0:tt * 4 + 1], s1), inc="dve")
                prog.op("dve", lambda v, tt=tt: v.tensor_copy(dbgstat[:, tt * 4 + 1:tt * 4 + 2], varc), inc="dve")
                prog.op("dve", lambda v, tt=tt: v.tensor_copy(dbgstat[:, tt * 4 + 2:tt * 4 + 3], rstd), inc="dve")
                mstat = prog.op("dve", lambda v, tt=tt: v.tensor_copy(dbgstat[:, tt * 4 + 3:tt * 4 + 4], meanc), inc="dve")
            m11 = prog.op("dve", lambda v, p=par: v.tensor_scalar(xnb, xt[p], rstd, nb, ALU.mult, ALU.add),
                          waits=[m10] + ([tr_done] if tr_done else []), inc="dve")
            for grp in range(4):
                bank = psA[:, (grp % 2) * 512:(grp % 2) * 512 + 512].bitcast(BF16)
                bank3 = _r2(bank, 8, 128)
                last_pe = None
                for k in range(8):
                    kd = grp * 8 + k
                    w = [m11]
                    if tt == 0 and grp == 0:
                        w.append(mil_dbc)
                    if tt > 0 or grp > 1:
                        w.append(cp_done[grp % 2])
                    last_pe = prog.op("pe",
                                      lambda t, kd=kd, bank3=bank3, k=k: t.transpose(
                                          bank3[:, k, :], xnb[:, kd * 128:(kd + 1) * 128], idb),
                                      waits=w, inc="pe")
                if tt == 0 and grp == 0:
                    cp_done = [None, None]
                cp_done[grp % 2] = prog.op(
                    "dve",
                    lambda v, bank3=bank3, g=grp, tt=tt: v.tensor_copy(
                        xnT3[:, g * 8:(g + 1) * 8, tt * 128:(tt + 1) * 128], bank3),
                    waits=[last_pe], inc="dve")
            tr_done = cp_done[(3) % 2]
            xn_done[par] = m11 if False else cp_done[1]


        ar.reset(m_p1a)
        p1a_done = [prog.last("dve"), prog.last("act"), prog.last("sp"), prog.last("pe")]
        wbuf = [ar.alloc(4096, BF16), ar.alloc(4096, BF16)]
        pfb = [ar.alloc(1024, F32) for _ in range(3)]
        pf = {"i": 0, "ld": [], "st": [], "fold": []}

        def plan_prefold_one():
            i = pf["i"]
            if i >= 256:
                return
            pf["i"] += 1
            ci, jb = i // 4, i % 4
            buf = pfb[i % 3]
            ld = prog.dma("act", buf, wout_h[ci * 128:(ci + 1) * 128, jb * 1024:(jb + 1) * 1024],
                          waits=[pf["st"][i - 3] if i >= 3 else None, c0] +
                                (p1a_done if i < 3 else []))
            fo = prog.op("dve", lambda v, buf=buf, ci=ci: v.tensor_scalar(
                buf, buf, nwc[:, ci:ci + 1], None, ALU.mult), waits=[ld])
            st = prog.dma("act", woutnw_sp[ci * 128:(ci + 1) * 128, jb * 1024:(jb + 1) * 1024],
                          buf, waits=[fo])
            pf["ld"].append(ld)
            pf["fold"].append(fo)
            pf["st"].append(st)

        ta = ar.alloc(2048, F32)
        tb = ar.alloc(2048, F32)
        convo = ar.alloc(2048, BF16)
        dtT_ap = tb

        psum = [psA[:, :], psB[:, :]]
        pe_after = {}
        cons_after = {}
        wdma = {}
        st_done = None
        cons_after[0] = cp_done[1]
        cons_after[1] = None

        for jt in range(NJT):
            plan_prefold_one()
            plan_prefold_one()
            par = jt % 2
            wdma[jt] = prog.dma("pool", _r2(wbuf[par], 32, 128), win_h[jt],
                                waits=[pe_after.get(jt - 2)] + (p1a_done if jt < 2 else []))
            wbuf3 = _r2(wbuf[par], 32, 128)
            last = None
            for ts in range(4):
                pbank = psum[par][:, ts * 512:(ts + 1) * 512]
                for kd in range(32):
                    w = []
                    if ts == 0 and kd == 0:
                        w = [wdma[jt], cons_after[par]] if cons_after[par] else [wdma[jt]]
                        if jt == 0:
                            w.append(cp_done[0])
                            w.append(cp_done[1])
                    last = prog.op("pe",
                                   lambda t, pbank=pbank, kd=kd, ts=ts, wb=wbuf3: t.matmul(
                                       pbank, wb[:, kd, :], xnT3[:, kd, ts * 512:(ts + 1) * 512],
                                       start=(kd == 0), stop=(kd == 31)),
                                   waits=w, inc="pe")
            pe_after[jt] = last

            if jt < 64:
                mact = prog.op("act",
                               lambda a, p=par: a.activation(convo, psum[p], AF.Silu),
                               waits=[last] + ([st_done] if st_done else []), inc="act")
                cons_after[par] = mact
                st_done = prog.dma("sp", gate_sp[jt * 128:(jt + 1) * 128, :], convo, waits=[mact])
            elif jt < 144:
                ct = jt - 64
                mw = [last]
                mtap = prog.op("dve",
                               lambda v, p=par, ct=ct: v.tensor_scalar(
                                   ta, psum[p], _r2(cwt, 80, 4)[:, ct, 3:4], None, ALU.mult),
                               waits=mw, inc="dve")
                for mm in range(1, 4):
                    mtap = prog.op("dve",
                                   lambda v, p=par, ct=ct, mm=mm: v.scalar_tensor_tensor(
                                       ta[:, mm:2048], psum[p][:, 0:2048 - mm],
                                       _r2(cwt, 80, 4)[:, ct, 3 - mm:4 - mm],
                                       ta[:, mm:2048], ALU.mult, ALU.add),
                                   inc="dve")
                cons_after[par] = mtap
                mact = prog.op("act",
                               lambda a, ct=ct: a.activation(convo, ta, AF.Silu, bias=cbt[:, ct:ct + 1]),
                               waits=[mtap] + ([st_done] if st_done else []), inc="act")
                st_done = prog.dma("sp", xbc_sp[ct * 128:(ct + 1) * 128, :], convo, waits=[mact])
            else:
                mcp = prog.op("dve", lambda v, p=par: v.tensor_copy(tb, psum[p]), waits=[last], inc="dve")
                cons_after[par] = mcp

        mdt = cons_after[144 % 2]
        ma = prog.op("act", lambda a: a.activation(ta, tb, AF.Abs, bias=dtbc), waits=[mdt], inc="act")
        me = prog.op("act", lambda a: a.activation(ta, ta, AF.Exp, scale=-1.0), waits=[ma], inc="act")
        ml = prog.op("act", lambda a: a.activation(ta, ta, AF.Ln, bias=cst[:, 1:2]), waits=[me], inc="act")
        mm_ = prog.op("dve", lambda v: v.tensor_scalar(tb, tb, dtbc, 0.0, ALU.add, ALU.max),
                      waits=[ml], inc="dve")
        msp = prog.op("dve", lambda v: v.tensor_tensor(tb, tb, ta, ALU.add), waits=[mm_], inc="dve")
        mcl = prog.op("dve", lambda v: v.tensor_scalar(dtv, tb, DT_MIN, DT_MAX, ALU.max, ALU.min),
                      waits=[msp], inc="dve")
        mda = prog.op("dve", lambda v: v.tensor_scalar(ta, dtv, acol, None, ALU.mult), waits=[mcl], inc="dve")
        mcs = None
        for c in range(NCH):
            mcs = prog.op("dve",
                          lambda v, c=c: v.tensor_tensor_scan(
                              cs[:, c * 128:(c + 1) * 128], ta[:, c * 128:(c + 1) * 128],
                              zer128, 0.0, ALU.add, ALU.add),
                          waits=[mda], inc="dve")
        if debug:
            prog.dma("sp", dbg["dtv"], dtv, waits=[mcl])
            prog.dma("sp", dbg["cs"], cs, waits=[mcs])
            prog.dma("sp", dbg["xnt"], xnT, waits=[mcs])
            prog.dma("sp", dbg["stat"], dbgstat, waits=[mstat])

        prog.barrier()
        ar.reset(m_p1)

        if upto == "p1":
            _finish(prog)
            _emit(nc, prog, sems)
            return nc

        p4s = {
            "wout_h": woutnw_sp, "x_h": x_h, "x1_sp": x1_sp, "nrm_sp": nrm_sp,
            "psA": psA, "psB": psB,
            "s1p": ar.alloc(128, F32), "s2p": ar.alloc(128, F32),
            "rstd16": ar.alloc(16, F32), "nb16": ar.alloc(16, F32),
            "m_stream": ar.mark(),
            "nt": [ar.alloc(1024, BF16) for _ in range(3)],
            "wb": [ar.alloc(1024, BF16) for _ in range(3)],
            "xres": [ar.alloc(512, F32) for _ in range(4)],
            "x1b": [ar.alloc(512, F32), ar.alloc(512, F32)],
            "dmp": ar.alloc(512, F32),
            "wlist": [], "pelist": [], "ntlist": [], "ntpe": [],
            "x1b_st": [None, None], "bank_cons": {}, "xres_cons": {},
            "mstat": None,
        }
        p4s["wready"] = pf["st"][-1] if pf["st"] else None
        m_p3 = ar.mark()

        def between(c, fr):
            if 8 <= c <= 15:
                emit_p4_js(prog, p4s, 0, c - 8, fr)
                fr["p4last"] = [p4s["bank_cons"].get(k) for k in range(4)] + [p4s["mstat"]]
        emit_p3(nc, prog, ar, psA, psB, locals(), between=between)
        prog.barrier()
        ar.reset(m_p3)
        ar.high = ar.nbytes
        if upto == "p3":
            _finish(prog)
            _emit(nc, prog, sems)
            return nc

        emit_p456(nc, prog, ar, psA, psB, locals(), p4s)
        _finish(prog)
        _emit(nc, prog, sems)
    return nc


def _finish(prog):
    prog.barrier()


def _emit(nc, prog, sems):
    with nc.Block() as block:
        prog.emit(nc, block, sems)


def emit_p3(nc, prog, ar, psA, psB, env, between=None):
    ALU_ = ALU
    dtv, cs = env["dtv"], env["cs"]
    mt, idf, onesf, dbc, cst = env["mt"], env["idf"], env["onesf"], env["dbc"], env["cst"]
    xbc_sp, gate_sp, nrm_sp = env["xbc_sp"], env["gate_sp"], env["nrm_sp"]
    prev = ar.alloc(IN, F32)
    prevb = ar.alloc(IN, BF16)
    xnat = ar.alloc(IN, BF16)
    ynat = ar.alloc(IN, BF16)
    nrm = ar.alloc(IN, BF16)
    gateb = nrm
    bt = ar.alloc(1024, BF16)
    ct_ = ar.alloc(1024, BF16)
    bnat = ar.alloc(1024, BF16)
    ssb = ar.alloc(1024, F32)
    t2m = [ar.alloc(1024, F32) for _ in range(2)]
    expb = [ar.alloc(1024, BF16) for _ in range(2)]
    scb = [ar.alloc(1024, BF16) for _ in range(2)]
    xdt8 = [ar.alloc(512, BF16) for _ in range(2)]
    xdd8 = [ar.alloc(512, BF16) for _ in range(2)]
    tsc = [ar.alloc(512, F32) for _ in range(2)]
    _t2 = ar.alloc(512, F32)
    tsc2 = [_t2, _t2]
    csTt = ar.alloc(128, F32)
    ncsT = ar.alloc(128, F32)
    ecsT = ar.alloc(128, F32)
    dtvT = ar.alloc(128, F32)
    ddwT = ar.alloc(128, F32)
    decay = ar.alloc(128, F32)
    ddw = ar.alloc(128, F32)
    cdc = ar.alloc(1, F32)
    cdrow = ar.alloc(128, F32)
    cdbc = ar.alloc(128, F32)
    sums = ar.alloc(8, F32)
    rstd8 = ar.alloc(8, F32)
    sqdump = t2m[0]

    bt3 = _r2(bt, 8, 128)
    ct3 = _r2(ct_, 8, 128)
    bnat3 = _r2(bnat, 8, 128)
    prev3 = _r2(prev, 128, 64)

    mz = prog.op("dve", lambda v: v.memset(prev, 0.0))

    fr = {"xnat": None, "gateb": None, "bt": None, "ct": None, "bnat": None,
          "nrm": None, "prevb_ready": None, "prev_scaled": None,
          "cb8": None, "nrm7": None}
    grp = {}
    p3cb = [prog.last("pe"), prog.last("dve"), prog.last("act")]

    for c in range(NCH):
        c0, c1 = c * 128, (c + 1) * 128
        cssl = cs[:, c0:c1]
        dtsl = dtv[:, c0:c1]
        last = c == NCH - 1
        ov = 8 <= c < 15
        eeng = "dve" if c >= 8 else "pool"
        cb = p3cb + (fr.get("p4last") or [] if last else [])

        if ov:
            T1 = [psB[:, 0:512]]
            YD = [psB[:, 512:1024]]
            YO = [psB[:, 1024:1536]]
            ST = [psB[:, 1536:2048]]
            a0 = psB[:, 0:128]
            a1 = psB[:, 128:256]
            sbanks = [psB[:, 512:1024], psB[:, 1024:1536], psB[:, 1536:2048]]
            npar = 1
        else:
            T1 = [psA[:, 0:512], psA[:, 512:1024]]
            YD = [psA[:, 1024:1536], psB[:, 512:1024]]
            YO = [psA[:, 1536:2048], psB[:, 1024:1536]]
            ST = [psB[:, 0:512], psB[:, 1536:2048]]
            a0 = psA[:, 0:128]
            a1 = psA[:, 512:640]
            sbanks = [psA[:, 1024:1536], psA[:, 1536:2048], psB[:, 512:1024]]
            npar = 2

        ldx = prog.dma("sp", xnat, xbc_sp[0:IN, c0:c1], waits=[fr["xnat"]], transpose=True, sem="dtr2")
        ldg = prog.dma("sp", gateb, gate_sp[0:IN, c0:c1], waits=[fr["nrm"], fr["gateb"]], transpose=True, sem="dtr2")
        ldb = prog.dma("act", bt3, xbc_sp[IN:IN + 1024, c0:c1].rearrange("(g n) l -> n g l", n=128),
                       waits=[fr["bt"]])
        ldc = prog.dma("act", ct3, xbc_sp[IN + 1024:IN + 2048, c0:c1].rearrange("(g n) l -> n g l", n=128),
                       waits=[fr["ct"]])
        ldbn = prog.dma("sp", bnat, xbc_sp[IN:IN + 1024, c0:c1], waits=[fr["bnat"]], transpose=True, sem="dtr2")

        m1 = prog.op("pe", lambda t, a0=a0, cssl=cssl: t.transpose(a0, cssl, idf), waits=cb)
        m2 = prog.op("dve", lambda v, a0=a0: v.tensor_copy(csTt, a0), waits=[m1] + cb)
        m3 = prog.op("dve", lambda v: v.tensor_scalar(ncsT, csTt, -1.0, None, ALU_.mult), waits=[m2])
        me1 = prog.op("act", lambda a: a.activation(ecsT, csTt, AF.Exp), waits=[m2] + cb)
        md1 = prog.op("act", lambda a, cssl=cssl, c1=c1: a.activation(
            decay, cssl, AF.Exp, scale=-1.0, bias=cs[:, c1 - 1:c1]), waits=[me1])
        mw1 = prog.op("dve", lambda v, dtsl=dtsl: v.tensor_tensor(ddw, dtsl, decay, ALU_.mult),
                      waits=[md1])
        if ov:
            m4 = prog.op("pe", lambda t, a1=a1, dtsl=dtsl: t.transpose(a1, dtsl, idf), waits=[m2] + cb)
        else:
            m4 = prog.op("pe", lambda t, a1=a1, dtsl=dtsl: t.transpose(a1, dtsl, idf), waits=cb)
        m5 = prog.op("dve", lambda v, a1=a1: v.tensor_copy(dtvT, a1), waits=[m4])
        m6 = prog.op("pe", lambda t, a0=a0: t.transpose(a0, ddw, idf), waits=[m2, mw1, m5 if ov else m2])
        m7 = prog.op("dve", lambda v, a0=a0: v.tensor_copy(ddwT, a0), waits=[m6])
        prep_done = [m5, m7]
        if not last:
            mc1 = prog.op("act", lambda a, c1=c1: a.activation(cdc, cs[:, c1 - 1:c1], AF.Exp),
                          waits=cb + [md1])
            mc2 = prog.op("pe", lambda t, a0=a0: t.transpose(a0[0:1, :], cdc, idf),
                          waits=[mc1, m7])
            mc3 = prog.op("dve", lambda v, a0=a0: v.tensor_copy(cdrow[0:1, :], a0[0:1, :]),
                          waits=[mc2])
            mc4 = prog.op("pe", lambda t, a0=a0: t.matmul(a0, onesf[0:1, :], cdrow[0:1, :],
                                                          start=True, stop=True),
                          waits=[mc3])
            mc5 = prog.op("dve", lambda v, a0=a0: v.tensor_copy(cdbc, a0), waits=[mc4])
            prep_done.append(mc5)
        mS = None
        nsb = len(sbanks)
        for g in range(8):
            pbg = sbanks[g % nsb][:, (g // nsb) * 128:(g // nsb + 1) * 128]
            w = [ldb, ldc] + cb
            if g >= nsb:
                w.append(grp.get(("ssb_ev", g - nsb)))
            mm1 = prog.op("pe", lambda t, pbg=pbg, g=g: t.matmul(pbg, bt3[:, g, :], ct3[:, g, :],
                                                                start=True, stop=True), waits=w)
            mm2 = prog.op("dve", lambda v, pbg=pbg, g=g: v.tensor_copy(ssb[:, g * 128:(g + 1) * 128], pbg),
                          waits=[mm1])
            grp[("ssb_ev", g)] = mm2
            mS = mm2
        if 0 < c < NCH - 1:
            cdb_b = _r2(cdbc, 128, 1)[:, :, 0:1].broadcast_to([128, 128, 64])
            mscale = prog.op("dve", lambda v, cdb_b=cdb_b: v.tensor_tensor(prev3, prev3, cdb_b, ALU_.mult),
                             waits=[mc5, fr["prevb_ready"]])
            fr["prev_scaled"] = mscale

        NGRP = 16
        lastF = None
        for it in range(NGRP + 2):
            if it >= 2:
                g8 = it - 2
                par = g8 % npar
                p2_ = g8 % 2
                gbc = g8 // 2
                h0 = g8 * 8
                ysl = _r2(ynat[:, h0 * 64:(h0 + 8) * 64], 8, 64)
                wF = [grp[("D", g8)], grp[("E", g8)], ldc, ldbn]
                if c > 0:
                    wF.append(fr["prevb_ready"])
                back = npar
                if grp.get(("H", g8 - back)) is not None:
                    wF.append(grp[("H", g8 - back)])
                if grp.get(("G", g8 - back)) is not None:
                    wF.append(grp[("G", g8 - back)])
                ydb, yob, stb = YD[par], YO[par], ST[par]
                mF = None
                for i in range(8):
                    h = g8 * 8 + i
                    mF = prog.op("pe", lambda t, i=i, ydb=ydb, p2_=p2_: t.matmul(
                        ydb[:, i * 64:(i + 1) * 64], scb[p2_][:, i * 128:(i + 1) * 128],
                        xdt8[p2_][:, i * 64:(i + 1) * 64], start=True, stop=True),
                        waits=wF if i == 0 else [])
                    if c > 0:
                        mF = prog.op("pe", lambda t, i=i, h=h, yob=yob, gbc=gbc: t.matmul(
                            yob[:, i * 64:(i + 1) * 64], ct3[:, gbc, :],
                            prevb[:, h * 64:(h + 1) * 64], start=True, stop=True))
                    if not last:
                        mF = prog.op("pe", lambda t, i=i, stb=stb, p2_=p2_, gbc=gbc: t.matmul(
                            stb[:, i * 64:(i + 1) * 64], bnat3[:, gbc, :],
                            xdd8[p2_][:, i * 64:(i + 1) * 64], start=True, stop=True))
                grp[("F", g8)] = mF
                lastF = mF
                mg0 = grp[("g0", g8)]
                if c == 0:
                    mG = prog.op("dve", lambda v, ysl=ysl, p2_=p2_, ydb=ydb: v.tensor_tensor(
                        ysl, _r2(tsc[p2_], 8, 64), _r2(ydb, 8, 64), ALU_.add),
                        waits=[mF, mg0])
                else:
                    ecs_b = ecsT[:, h0:h0 + 8].unsqueeze(2).broadcast_to([128, 8, 64])
                    mg1 = prog.op("dve", lambda v, yob=yob, p2_=p2_, ecs_b=ecs_b: v.tensor_tensor(
                        _r2(tsc2[p2_], 8, 64), _r2(yob, 8, 64), ecs_b, ALU_.mult),
                        waits=[mF, me1, grp.get(("G", g8 - 1))])
                    mg2 = prog.op("dve", lambda v, ydb=ydb, p2_=p2_: v.tensor_tensor(
                        _r2(tsc[p2_], 8, 64), _r2(tsc[p2_], 8, 64), _r2(ydb, 8, 64), ALU_.add),
                        waits=[mg0, mF])
                    mG = prog.op("dve", lambda v, ysl=ysl, p2_=p2_: v.tensor_tensor(
                        ysl, _r2(tsc[p2_], 8, 64), _r2(tsc2[p2_], 8, 64), ALU_.add),
                        waits=[mg1, mg2])
                grp[("G", g8)] = mG
                if not last:
                    wH = [mF]
                    if c == 0:
                        wH.append(mz)
                    elif fr["prev_scaled"] is not None:
                        wH.append(fr["prev_scaled"])
                    mH = prog.op("dve", lambda v, g8=g8, stb=stb: v.tensor_tensor(
                        prev[:, g8 * 512:(g8 + 1) * 512], prev[:, g8 * 512:(g8 + 1) * 512],
                        stb, ALU_.add), waits=wH)
                else:
                    mH = None
                grp[("H", g8)] = mH
            if it < NGRP:
                g8 = it
                p2_ = g8 % 2
                gbc = g8 // 2
                h0 = g8 * 8
                mt_b = mt.unsqueeze(1).broadcast_to([128, 8, 128])
                cs_b = csTt[:, h0:h0 + 8].unsqueeze(2).broadcast_to([128, 8, 128])
                mA = prog.op("dve", lambda v, p2_=p2_, mt_b=mt_b, cs_b=cs_b: v.tensor_tensor(
                    _r2(t2m[p2_], 8, 128), mt_b, cs_b, ALU_.add),
                    waits=[m2, m3, grp.get(("B", g8 - 2))])
                grp[("A", g8)] = mA
                if len(T1) == 2:
                    wB = [mA] + (prep_done if g8 == 0 else [])
                    if grp.get(("C", g8 - 1)) is not None:
                        wB.append(grp[("C", g8 - 1)])
                    mB = None
                    for i in range(8):
                        t1slot = T1[i // 4][:, (i % 4) * 128:(i % 4 + 1) * 128]
                        mB = prog.op("pe", lambda t, i=i, p2_=p2_, t1slot=t1slot: t.transpose(
                            t1slot, t2m[p2_][:, i * 128:(i + 1) * 128], idf),
                            waits=wB if i == 0 else [])
                    mC = None
                    for i in range(8):
                        h = g8 * 8 + i
                        t1slot = T1[i // 4][:, (i % 4) * 128:(i % 4 + 1) * 128]
                        mC = prog.op("act", lambda a, i=i, h=h, p2_=p2_, t1slot=t1slot: a.activation(
                            expb[p2_][:, i * 128:(i + 1) * 128], t1slot, AF.Exp, bias=ncsT[:, h:h + 1]),
                            waits=[mB, m3] if i == 0 else [])
                else:
                    mC = None
                    for wv in range(2):
                        wB = [mA] + (prep_done if (g8 == 0 and wv == 0) else [])
                        wB.append(mC if wv == 1 else (grp.get(("C", g8 - 1))))
                        mB = None
                        for i in range(wv * 4, wv * 4 + 4):
                            t1slot = T1[0][:, (i % 4) * 128:(i % 4 + 1) * 128]
                            mB = prog.op("pe", lambda t, i=i, p2_=p2_, t1slot=t1slot: t.transpose(
                                t1slot, t2m[p2_][:, i * 128:(i + 1) * 128], idf),
                                waits=wB if i == wv * 4 else [])
                        for i in range(wv * 4, wv * 4 + 4):
                            h = g8 * 8 + i
                            t1slot = T1[0][:, (i % 4) * 128:(i % 4 + 1) * 128]
                            mC = prog.op("act", lambda a, i=i, h=h, p2_=p2_, t1slot=t1slot: a.activation(
                                expb[p2_][:, i * 128:(i + 1) * 128], t1slot, AF.Exp,
                                bias=ncsT[:, h:h + 1]),
                                waits=[mB, m3] if i == wv * 4 else [])
                grp[("C", g8)] = mC
                sb_b = _r2(ssb[:, gbc * 128:(gbc + 1) * 128], 1, 128).broadcast_to([128, 8, 128])
                mD = prog.op("dve", lambda v, p2_=p2_, sb_b=sb_b: v.tensor_tensor(
                    _r2(scb[p2_], 8, 128), _r2(expb[p2_], 8, 128), sb_b, ALU_.mult),
                    waits=[mC, mS, grp.get(("F", g8 - 2))])
                grp[("D", g8)] = mD
                wE = [ldx, m5, m7, grp.get(("F", g8 - 2))]
                xsl = _r2(xnat[:, h0 * 64:(h0 + 8) * 64], 8, 64)
                dtv_b = dtvT[:, h0:h0 + 8].unsqueeze(2).broadcast_to([128, 8, 64])
                mE = prog.op(eeng, lambda v, p2_=p2_, xsl=xsl, dtv_b=dtv_b: v.tensor_tensor(
                    _r2(xdt8[p2_], 8, 64), xsl, dtv_b, ALU_.mult), waits=wE)
                if not last:
                    ddw_b = ddwT[:, h0:h0 + 8].unsqueeze(2).broadcast_to([128, 8, 64])
                    mE = prog.op(eeng, lambda v, p2_=p2_, xsl=xsl, ddw_b=ddw_b: v.tensor_tensor(
                        _r2(xdd8[p2_], 8, 64), xsl, ddw_b, ALU_.mult), waits=[mE])
                grp[("E", g8)] = mE
                d_b = dbc[:, h0:h0 + 8].unsqueeze(2).broadcast_to([128, 8, 64])
                grp[("g0", g8)] = prog.op(eeng, lambda v, xsl=xsl, d_b=d_b, p2_=p2_: v.tensor_tensor(
                    _r2(tsc[p2_], 8, 64), xsl, d_b, ALU_.mult),
                    waits=[ldx, grp.get(("G", g8 - 2)), mE])

        lastG = grp[("G", NGRP - 1)]
        lastH = grp[("H", NGRP - 1)]
        if not last:
            fr["prevb_ready"] = prog.op("dve", lambda v: v.tensor_copy(prevb, prev),
                                        waits=[lastH])
        dbgd = env.get("dbg") or {}
        mdbg = None
        if dbgd and c == NCH - 1:
            prog.dma("sp", dbgd["p15"][:, 0:128], dtvT, waits=[lastG])
            prog.dma("sp", dbgd["p15"][:, 128:256], csTt, waits=[lastG])
            prog.dma("sp", dbgd["p15"][:, 256:384], ncsT, waits=[lastG])
            prog.dma("sp", dbgd["p15"][:, 384:512], ecsT, waits=[lastG])
            prog.dma("sp", dbgd["p15"][:, 512:640], ssb[:, 896:1024], waits=[lastG])
            mdbg = prog.dma("sp", dbgd["y15"], ynat, waits=[lastG])
            prog.dma("sp", dbgd["scb15"], scb[0], waits=[lastG])
            prog.dma("sp", dbgd["xdt15"], xdt8[0], waits=[lastG])
            prog.dma("sp", dbgd["prevb15"], prevb, waits=[lastG])
        myg = prog.op("dve", lambda v: v.tensor_tensor(ynat, ynat, gateb, ALU_.mult),
                      waits=[lastG, ldg, mdbg])
        fr["gateb"] = myg
        fr["xnat"] = grp[("g0", NGRP - 1)]
        fr["bt"] = mS
        fr["ct"] = grp[("F", NGRP - 1)]
        fr["bnat"] = grp[("F", NGRP - 1)]
        msq_ = None
        for g in range(8):
            msq_ = prog.op("act", lambda a, g=g: a.activation(
                sqdump, ynat[:, g * 1024:(g + 1) * 1024], AF.Square, accum_out=sums[:, g:g + 1]),
                waits=[myg] if g == 0 else [])
        mln = prog.op("act", lambda a: a.activation(rstd8, sums, AF.Ln, scale=1.0 / 1024.0,
                                                    bias=cst[:, 0:1]), waits=[msq_])
        mex = prog.op("act", lambda a: a.activation(rstd8, rstd8, AF.Exp, scale=-0.5),
                      waits=[mln])
        mn = None
        for g in range(8):
            mn = prog.op("dve", lambda v, g=g: v.tensor_scalar(
                nrm[:, g * 1024:(g + 1) * 1024], ynat[:, g * 1024:(g + 1) * 1024],
                rstd8[:, g:g + 1], None, ALU_.mult),
                waits=([mex, fr["nrm"], myg] if g == 0 else []))
        fr["nrm"] = prog.dma("act", nrm_sp[c0:c1, :], nrm, waits=[mn], sem="dspl")
        if c == 7:
            fr["nrm7"] = fr["nrm"]
            fr["cb8"] = [lastF, mn, mex, grp[("g0", NGRP - 1)]]
        p3cb = [lastF, mn, mex, grp[("g0", NGRP - 1)], grp[("E", NGRP - 1)]]
        if last:
            fr["cb15"] = p3cb
        grp = {}
        if between is not None:
            between(c, fr)
def emit_p4_js(prog, S, half, js, fr=None):
    wout_h, x_h, x1_sp, nrm_sp = S["wout_h"], S["x_h"], S["x1_sp"], S["nrm_sp"]
    psA, psB = S["psA"], S["psB"]
    jg = half * 8 + js
    if half == 0:
        passes = [[0, 1, 2, 3], [4, 5, 6, 7]]
        bank_of = lambda k: (k, psA[:, k * 512:(k + 1) * 512])
    else:
        passes = [[0, 1, 2, 3, 4, 5, 6, 7]]
        bank_of = lambda k: (k, psA[:, k * 512:(k + 1) * 512] if k < 4
                             else psB[:, (k - 4) * 512:(k - 3) * 512])
    start = list(fr["cb8"]) if (fr is not None and half == 0 and js == 0) else []
    if fr is not None and half == 0 and js == 7:
        start += fr.get("cb15") or []
    ntgate = [fr["nrm7"]] if (fr is not None and half == 0 and js == 0) else []
    for pi, tts in enumerate(passes):
        nb = len(tts)
        trow = half * 1024 + tts[0] * 128
        twid = nb * 128
        for k, tt in enumerate(tts[:4]):
            tg = half * 8 + tt
            S[("xl", k)] = prog.dma(
                "sp", S["xres"][k % 4], x_h[tg * 128:(tg + 1) * 128, js * 512:(js + 1) * 512],
                waits=[S["xres_cons"].get(k % 4)])
        mlast = None
        for cp in range(32):
            wi = len(S["wlist"])
            wld = prog.dma("pool", _r2(S["wb"][wi % 3], 2, 512),
                           wout_h[cp * 256:(cp + 1) * 256, js * 512:(js + 1) * 512]
                           .rearrange("(b p) j -> p b j", p=128),
                           waits=[S["wlist"][wi - 3] if wi >= 3 else None,
                                  S["pelist"][wi - 3] if wi >= 3 else None,
                                  S.get("wready") if wi < 3 else None])
            S["wlist"].append(wld)
            S["pelist"].append(None)
            for b in range(2):
                ct = cp * 2 + b
                ni = len(S["ntlist"])
                ntld = prog.dma("sp", S["nt"][ni % 3][:, 0:twid],
                                nrm_sp[trow:trow + twid, ct * 128:(ct + 1) * 128],
                                transpose=True, sem="dtr",
                                waits=[S["ntlist"][ni - 3] if ni >= 3 else None,
                                       S["ntpe"][ni - 3] if ni >= 3 else None] + ntgate)
                S["ntlist"].append(ntld)
                mm = None
                for k, tt in enumerate(tts):
                    bidx, bank = bank_of(k)
                    w = [wld, ntld] if k == 0 else []
                    if ct == 0:
                        w = [wld, ntld, S["bank_cons"].get(bidx)] + start
                    mm = prog.op("pe", lambda t, bank=bank, k=k, ni=ni, wi=wi, b=b, ct=ct: t.matmul(
                        bank, S["nt"][ni % 3][:, k * 128:(k + 1) * 128],
                        _r2(S["wb"][wi % 3], 2, 512)[:, b, :],
                        start=(ct == 0), stop=(ct == 63)),
                        waits=[x for x in w if x is not None])
                S["ntpe"].append(mm)
            S["pelist"][wi] = mm
            mlast = mm
        for k, tt in enumerate(tts):
            bidx, bank = bank_of(k)
            tg = half * 8 + tt
            m1 = prog.op("dve", lambda v, bank=bank, k=k: v.tensor_tensor(
                S["x1b"][k % 2], bank, S["xres"][k % 4], ALU.add),
                waits=[mlast, S[("xl", k)], S["x1b_st"][k % 2]])
            S["bank_cons"][bidx] = m1
            S["xres_cons"][k % 4] = m1
            if k + 4 < len(tts):
                tg2 = half * 8 + tts[k + 4]
                S[("xl", k + 4)] = prog.dma(
                    "sp", S["xres"][k % 4], x_h[tg2 * 128:(tg2 + 1) * 128, js * 512:(js + 1) * 512],
                    waits=[m1])
            col = tg * 8 + js
            m2 = prog.op("act", lambda a, k=k, col=col: a.activation(
                S["dmp"], S["x1b"][k % 2], AF.Copy, accum_out=S["s1p"][:, col:col + 1]),
                waits=[m1])
            m3 = prog.op("act", lambda a, k=k, col=col: a.activation(
                S["dmp"], S["x1b"][k % 2], AF.Square, accum_out=S["s2p"][:, col:col + 1]),
                waits=[m2])
            S["mstat"] = m3
            S["x1b_st"][k % 2] = prog.dma(
                "sp", x1_sp[tg * 128:(tg + 1) * 128, js * 512:(js + 1) * 512],
                S["x1b"][k % 2], waits=[m3])


def emit_p456(nc, prog, ar, psA, psB, env, S):
    idb, onesf, cst = env["idb"], env["onesf"], env["cst"]
    x1_sp, y_h = env["x1_sp"], env["y_h"]
    wk_h = env["wk_h"]
    fgate_h, fbias_h = env["fgate_h"], env["fbias_h"]
    lA_h, lB_h, lAs_h, lBs_h = env["lA_h"], env["lB_h"], env["lAs_h"], env["lBs_h"]
    s1p, s2p, rstd16, nb16 = S["s1p"], S["s2p"], S["rstd16"], S["nb16"]

    for js in range(8):
        emit_p4_js(prog, S, 1, js)
    mstat = S["mstat"]

    m_fin = ar.mark()
    sc1 = ar.alloc(1, F32)
    sc2 = ar.alloc(1, F32)
    sc3 = ar.alloc(1, F32)
    mr = None
    for tg in range(16):
        w = [mstat] if tg == 0 else [mr]
        a1 = prog.op("dve", lambda v, tg=tg: v.tensor_reduce(
            sc1, s1p[:, tg * 8:(tg + 1) * 8], mybir.AxisListType.X, ALU.add), waits=w)
        a2 = prog.op("dve", lambda v, tg=tg: v.tensor_reduce(
            sc2, s2p[:, tg * 8:(tg + 1) * 8], mybir.AxisListType.X, ALU.add), waits=[a1])
        a3 = prog.op("dve", lambda v: v.tensor_scalar(sc1, sc1, 1.0 / D, None, ALU.mult), waits=[a2])
        a4 = prog.op("dve", lambda v: v.tensor_tensor(sc3, sc1, sc1, ALU.mult), waits=[a3])
        a5 = prog.op("dve", lambda v, tg=tg: v.scalar_tensor_tensor(
            rstd16[:, tg:tg + 1], sc2, 1.0 / D, sc3, ALU.mult, ALU.subtract), waits=[a4])
        a6 = prog.op("act", lambda a, tg=tg: a.activation(
            rstd16[:, tg:tg + 1], rstd16[:, tg:tg + 1], AF.Ln, bias=cst[:, 0:1]), waits=[a5])
        a7 = prog.op("act", lambda a, tg=tg: a.activation(
            rstd16[:, tg:tg + 1], rstd16[:, tg:tg + 1], AF.Exp, scale=-0.5), waits=[a6])
        mr = prog.op("dve", lambda v, tg=tg: v.scalar_tensor_tensor(
            nb16[:, tg:tg + 1], sc1, -1.0, rstd16[:, tg:tg + 1], ALU.mult, ALU.mult), waits=[a7])

    prog.barrier()
    ar.reset(S["m_stream"])

    xn2T = ar.alloc(32 * 2048, BF16)
    xn2T3 = _r2(xn2T, 32, 2048)
    AT = ar.alloc(32 * 64, BF16)
    AT3 = _r2(AT, 32, 64)
    BS = ar.alloc(4096, BF16)
    AxS = ar.alloc(2048, BF16)
    gbc = ar.alloc(16, F32)
    m_p5s = ar.mark()
    x1t = [ar.alloc(D, F32), ar.alloc(D, F32)]
    xn2b = ar.alloc(D, BF16)

    tr_done = None
    m1hist = {}
    for tg in range(16):
        mload = prog.dma("sp", x1t[tg % 2], x1_sp[tg * 128:(tg + 1) * 128, :],
                         waits=[m1hist.get(tg - 2)])
        m1 = prog.op("dve", lambda v, tg=tg: v.tensor_scalar(
            xn2b, x1t[tg % 2], rstd16[:, tg:tg + 1], nb16[:, tg:tg + 1], ALU.mult, ALU.add),
            waits=[mload, tr_done])
        m1hist[tg] = m1
        cp_done = [None, None]
        for grp_ in range(4):
            bank = psA[:, (grp_ % 2) * 512:(grp_ % 2) * 512 + 512].bitcast(BF16)
            bank3 = _r2(bank, 8, 128)
            lastpe = None
            for k in range(8):
                kd = grp_ * 8 + k
                w = [m1]
                if grp_ > 1:
                    w.append(cp_done[grp_ % 2])
                lastpe = prog.op("pe", lambda t, kd=kd, bank3=bank3, k=k: t.transpose(
                    bank3[:, k, :], xn2b[:, kd * 128:(kd + 1) * 128], idb),
                    waits=w if k == 0 else [])
            cp_done[grp_ % 2] = prog.op("dve", lambda v, bank3=bank3, g=grp_, tg=tg: v.tensor_copy(
                xn2T3[:, g * 8:(g + 1) * 8, tg * 128:(tg + 1) * 128], bank3), waits=[lastpe])
        tr_done = cp_done[1]

    ar.reset(m_p5s)
    aacc = ar.alloc(1024, F32)
    lbufA = ar.alloc(1024, F32)
    lbufB = ar.alloc(4096, F32)
    bacc = ar.alloc(4096, F32)

    ldg = prog.dma("act", gbc[0:1, 0:16], fgate_h, sem="dc5")
    pg = psB[:, 0:16]
    mg1 = prog.op("pe", lambda t: t.matmul(pg, onesf[0:1, :], gbc[0:1, 0:16], start=True, stop=True),
                  waits=[ldg, tr_done])
    mg2 = prog.op("dve", lambda v: v.tensor_copy(gbc, pg), waits=[mg1])
    macc = None
    for n in range(16):
        lda = prog.dma("sp", _r2(lbufA, 32, 32), lA_h[n],
                       waits=[macc if n > 0 else None] + ([tr_done, m1hist[15]] if n == 0 else []))
        if n == 0:
            macc = prog.op("dve", lambda v: v.tensor_scalar(
                aacc, lbufA, gbc[:, 0:1], None, ALU.mult), waits=[lda, mg2])
        else:
            macc = prog.op("dve", lambda v, n=n: v.scalar_tensor_tensor(
                aacc, lbufA, gbc[:, n:n + 1], aacc, ALU.mult, ALU.add),
                waits=[lda, macc])
    mat1 = prog.op("dve", lambda v: v.tensor_copy(AT3[:, :, 0:32], _r2(aacc, 32, 32)), waits=[macc])
    ldas = prog.dma("sp", _r2(lbufA, 32, 32), lAs_h, waits=[macc])
    mat2 = prog.op("dve", lambda v: v.tensor_copy(AT3[:, :, 32:64], _r2(lbufA, 32, 32)),
                   waits=[ldas, mat1])
    mbcc = None
    for n in range(16):
        ldb_ = prog.dma("pool", lbufB[0:32, :], lB_h[n],
                        waits=[mbcc if n > 0 else None] + ([tr_done, m1hist[15]] if n == 0 else []))
        if n == 0:
            mbcc = prog.op("dve", lambda v: v.tensor_scalar(
                bacc[0:32, :], lbufB[0:32, :], gbc[0:32, 0:1], None, ALU.mult),
                waits=[ldb_, mg2])
        else:
            mbcc = prog.op("dve", lambda v, n=n: v.scalar_tensor_tensor(
                bacc[0:32, :], lbufB[0:32, :], gbc[0:32, n:n + 1], bacc[0:32, :],
                ALU.mult, ALU.add), waits=[ldb_, mbcc])
    mbs1 = prog.op("dve", lambda v: v.tensor_copy(BS[0:32, :], bacc[0:32, :]), waits=[mbcc])
    ldbs = prog.dma("pool", BS[32:64, :], lBs_h, sem="dc6", waits=[mbs1])
    ldfb = prog.dma("pool", BS[64:65, :], fbias_h, sem="dc6", waits=[mbs1])
    mones = prog.op("dve", lambda v: v.memset(AxS[64:65, :], 1.0))
    axd = None
    for strip in range(4):
        pax = psB[:, 512 + (strip % 2) * 512: 512 + (strip % 2) * 512 + 512]
        ml = None
        for kd in range(32):
            w = [mat2, tr_done] if kd == 0 else []
            if kd == 0 and axd is not None and strip >= 2:
                w.append(axd)
            ml = prog.op("pe", lambda t, kd=kd, strip=strip, pax=pax: t.matmul(
                pax[0:64, :], AT3[:, kd, :], xn2T3[:, kd, strip * 512:(strip + 1) * 512],
                start=(kd == 0), stop=(kd == 31)), waits=w)
        axd = prog.op("dve", lambda v, strip=strip, pax=pax: v.tensor_copy(
            AxS[0:64, strip * 512:(strip + 1) * 512], pax[0:64, :]), waits=[ml])

    ar.reset(m_p5s)
    kb = [ar.alloc(1024, BF16) for _ in range(4)]
    x1r = [ar.alloc(512, F32) for _ in range(8)]
    ob = [ar.alloc(512, F32), ar.alloc(512, F32)]
    pbank = [psA[:, i * 512:(i + 1) * 512] for i in range(4)] + \
            [psB[:, i * 512:(i + 1) * 512] for i in range(4)]
    evac_hist = {}
    bank_cons = [None] * 8
    kd_ = {}
    pe_ct = {}
    mlast = None
    ecnt = 0
    for half in range(2):
        for js in range(8):
            ji = half * 8 + js
            for tt in range(8):
                tg = half * 8 + tt
                evac_hist[("p6xl", ji, tt)] = prog.dma(
                    "sp", x1r[tt], x1_sp[tg * 128:(tg + 1) * 128, js * 512:(js + 1) * 512],
                    waits=[evac_hist.get(("p6x", ji - 1, tt))] +
                          ([mat2, axd, mbs1] if ji == 0 else []))
            for cp in range(16):
                par = cp % 4
                kd_[ji, cp] = prog.dma(
                    "pool", _r2(kb[par], 2, 512),
                    wk_h[cp * 256:(cp + 1) * 256, js * 512:(js + 1) * 512]
                    .rearrange("(b p) j -> p b j", p=128),
                    waits=[kd_.get((ji, cp - 4), kd_.get((ji - 1, 12 + cp))),
                           pe_ct.get((ji, cp - 4), pe_ct.get((ji - 1, 12 + cp))),
                           ldbs if (ji == 0 and cp == 0) else None])
                for b in range(2):
                    ct = cp * 2 + b
                    for tt in range(8):
                        tg = half * 8 + tt
                        w = []
                        if tt == 0:
                            w = [kd_[ji, cp]]
                        if ct == 0:
                            w = [kd_[ji, cp], bank_cons[tt], tr_done]
                            if ji == 0:
                                w += [mg2, axd]
                        mlast = prog.op("pe", lambda t, ct=ct, tt=tt, tg=tg, par=par, b=b: t.matmul(
                            pbank[tt], xn2T3[:, ct, tg * 128:(tg + 1) * 128],
                            _r2(kb[par], 2, 512)[:, b, :],
                            start=(ct == 0), stop=False), waits=[x for x in w if x is not None])
                pe_ct[ji, cp] = mlast
            for tt in range(8):
                tg = half * 8 + tt
                w = [axd, mones, ldbs, ldfb] if (ji == 0 and tt == 0) else []
                mlast = prog.op("pe", lambda t, tt=tt, tg=tg, js=js: t.matmul(
                    pbank[tt], AxS[0:65, tg * 128:(tg + 1) * 128], BS[0:65, js * 512:(js + 1) * 512],
                    start=False, stop=True), waits=w)
            mmdone = mlast
            for tt in range(8):
                tg = half * 8 + tt
                par = ecnt % 2
                m1 = prog.op("dve", lambda v, tt=tt, par=par: v.tensor_tensor(
                    ob[par], pbank[tt], x1r[tt], ALU.add),
                    waits=[mmdone, evac_hist[("p6xl", ji, tt)], evac_hist.get(("p6s", ecnt - 2))])
                bank_cons[tt] = m1
                evac_hist[("p6x", ji, tt)] = m1
                evac_hist[("p6s", ecnt)] = prog.dma(
                    "sp", y_h[tg * 128:(tg + 1) * 128, js * 512:(js + 1) * 512], ob[par], waits=[m1])
                ecnt += 1

def _prep_core_inputs(inputs, b):
    f32 = np.float32
    win = inputs["L0_attn_in_proj_w"]
    d = {
        "x": np.ascontiguousarray(inputs["L0_attn_hidden_states"][b]).astype(f32, copy=False),
        "win": np.ascontiguousarray(win.reshape(NJT, 128, 32, 128).transpose(0, 3, 2, 1)),
        "cw": np.ascontiguousarray(inputs["L0_attn_conv_w"].reshape(80, 128, 4).transpose(1, 0, 2)),
        "cb": np.ascontiguousarray(inputs["L0_attn_conv_b"].reshape(80, 128).T),
        "dtb": np.ascontiguousarray(inputs["L0_attn_dt_bias"].reshape(128, 1)),
        "dvec": np.ascontiguousarray(inputs["L0_attn_D"].reshape(1, 128)),
        "nw": np.ascontiguousarray(inputs["L0_attn_norm_w"].reshape(64, 128).T),
        "wout": np.ascontiguousarray(inputs["L0_attn_out_proj_w"].T),
        "wk": np.ascontiguousarray(inputs["L0_ffn_kernel"]),
        "fgate": np.ascontiguousarray(inputs["L0_ffn_gate"][b].reshape(1, 16)),
        "fbias": np.ascontiguousarray(inputs["L0_ffn_bias"].reshape(1, D)),
        "lA": np.ascontiguousarray(inputs["L0_ffn_lora_A"].transpose(0, 2, 1).reshape(16, 32, 128, 32)
                                   .transpose(0, 2, 1, 3).reshape(16, 128, 1024)),
        "lB": np.ascontiguousarray(inputs["L0_ffn_lora_B"].transpose(0, 2, 1)),
        "lAs": np.ascontiguousarray(inputs["L0_ffn_lora_A_shared"].T.reshape(32, 128, 32)
                                    .transpose(1, 0, 2).reshape(128, 1024)),
        "lBs": np.ascontiguousarray(inputs["L0_ffn_lora_B_shared"].T),
        "idf": np.eye(128, dtype=f32),
        "idb": np.eye(128, dtype=BFNP),
        "mt": np.where(np.arange(128)[None, :] <= np.arange(128)[:, None], 0.0, NEG).astype(f32),
        "acol": -(np.arange(1, 129, dtype=f32)).reshape(128, 1),
        "onesf": np.ones((1, 128), dtype=f32),
    }
    return d



def kernel(**inputs):
    nc = build(upto="full", debug=False)
    in_maps = [_prep_core_inputs(inputs, b) for b in range(8)]
    res = run_bass_kernel_spmd(nc, in_maps, core_ids=list(range(8)))
    out = np.stack([np.asarray(res.results[b]["y"]) for b in range(8)], axis=0)
    return out.astype(np.float32, copy=False)


# version 08
# speedup vs baseline: 1.1351x; 1.1351x over previous
import numpy as np
import ml_dtypes
from contextlib import ExitStack
import concourse.bass as bass
import concourse.mybir as mybir
from concourse.bass_utils import run_bass_kernel_spmd

F32 = mybir.dt.float32
BF16 = mybir.dt.bfloat16
I8 = mybir.dt.int8
AF = mybir.ActivationFunctionType
ALU = mybir.AluOpType
BFNP = ml_dtypes.bfloat16

T = 2048
D = 4096
J = 18560
IN = 8192
CV = 10240
NH = 128
HD = 64
NG = 8
SS = 128
L = 128
NCH = 16
NJT = J // 128
EPS = 1e-5
DT_MIN, DT_MAX = 1e-3, 100.0
NEG = -1e30

ENGINES = ("pe", "dve", "act", "sp", "pool")
_BLOCK_ATTR = {"pe": "tensor", "dve": "vector", "act": "scalar", "sp": "sync", "pool": "gpsimd"}
DMA_SEM = {"sp": "dsp", "act": "dact", "pool": "dpool"}
TOTAL_SEMS = ("dc0", "dc5", "dc6")
ALL_SEMS = list(ENGINES) + list(DMA_SEM.values()) + list(TOTAL_SEMS)


class _Op:
    __slots__ = ("fn", "sem", "dma", "marked", "waits", "val")

    def __init__(self, fn, sem, dma, waits):
        self.fn = fn
        self.sem = sem
        self.dma = dma
        self.marked = dma
        self.waits = waits
        self.val = None


class Prog:

    def __init__(self):
        self.ops = {e: [] for e in ENGINES}
        self.order = []
        self.barriers = []

    def op(self, eng, fn, waits=(), inc=None):
        ws = [w for w in waits if w is not None]
        o = _Op(fn, eng, False, ws)
        for w in ws:
            if isinstance(w, _Op):
                w.marked = True
        self.ops[eng].append(o)
        self.order.append((eng, o))
        return o

    def dma(self, q, out, in_, waits=(), transpose=False, sem=None):
        ws = [w for w in waits if w is not None]
        if transpose:
            fn = lambda e, out=out, in_=in_: e.dma_start_transpose(out, in_)
        else:
            fn = lambda e, out=out, in_=in_: e.dma_start(out=out, in_=in_)
        o = _Op(fn, sem or DMA_SEM[q], True, ws)
        for w in ws:
            if isinstance(w, _Op):
                w.marked = True
        self.ops[q].append(o)
        self.order.append((q, o))
        return o

    def last(self, eng):
        for o in reversed(self.ops[eng]):
            if o.fn is not None:
                return o
        return None

    def barrier(self):
        lasts = [self.last(e) for e in ENGINES]
        lasts = [o for o in lasts if o is not None]
        for o in lasts:
            o.marked = True
        for e in ENGINES:
            self.op(e, None, waits=list(lasts))

    def finalize(self):
        cnt = {s: 0 for s in ALL_SEMS}
        for (eng, o) in self.order:
            if o.marked:
                assert o.fn is not None or o.dma, "marked no-op would never increment"
                cnt[o.sem] += 16 if o.dma else 1
                o.val = cnt[o.sem]
        self.total = cnt

    def emit(self, nc, block, sems):
        self.finalize()
        for e in ENGINES:
            attr = _BLOCK_ATTR[e]
            ops = self.ops[e]

            def section(h, ops=ops, e=e):
                hwm = {s: 0 for s in ALL_SEMS}
                for o in ops:
                    for w in o.waits:
                        assert isinstance(w, _Op) and w.marked and w.val is not None
                        s = w.sem
                        v = self.total[s] if s in TOTAL_SEMS else w.val
                        if hwm[s] < v:
                            hwm[s] = v
                            h.wait_ge(sems[s], v)
                    if o.fn is not None:
                        ins = o.fn(h)
                        if o.marked and not o.dma:
                            ins.then_inc(sems[o.sem], 1)
                        elif o.dma:
                            ins.then_inc(sems[o.sem], 16)

            getattr(block, attr)(section)


class Arena:

    def __init__(self, ap_int8):
        self.base = ap_int8
        self.nbytes = ap_int8.shape[1]
        self.off = 0
        self.high = self.nbytes
        self.peak = 0

    def alloc(self, free_elems, dtype, align=64):
        esz = {F32: 4, BF16: 2, I8: 1}[dtype]
        self.off = (self.off + align - 1) // align * align
        nb = free_elems * esz
        ap = self.base[:, self.off:self.off + nb]
        self.off += nb
        self.peak = max(self.peak, self.off)
        assert self.off <= self.high, f"arena overflow {self.off} > {self.high}"
        if dtype == I8:
            return ap
        return ap.bitcast(dtype)

    def alloc_high(self, free_elems, dtype, align=64):
        esz = {F32: 4, BF16: 2, I8: 1}[dtype]
        nb = (free_elems * esz + align - 1) // align * align
        self.high -= nb
        assert self.off <= self.high, "arena overflow (high)"
        ap = self.base[:, self.high:self.high + free_elems * esz]
        return ap if dtype == I8 else ap.bitcast(dtype)

    def alloc2(self, p, f, dtype, align=512):
        ap = self.alloc(f, dtype, align)
        return ap.rearrange("p (a f) -> p a f", a=1)[:, 0, :] if False else ap

    def mark(self):
        return self.off

    def reset(self, mark):
        self.off = mark


def _r2(ap, *shape):
    s = "p (" + " ".join(f"a{i}" for i in range(len(shape))) + ") -> p " + " ".join(f"a{i}" for i in range(len(shape)))
    return ap.rearrange(s, **{f"a{i}": n for i, n in enumerate(shape)})


def build(upto="full", debug=False):
    nc = bass.Bass("TRN2", target_bir_lowering=False)
    kind_spill = "ExternalOutput" if debug else "Internal"

    x_h = nc.dram_tensor("x", [T, D], F32, kind="ExternalInput").ap()
    win_h = nc.dram_tensor("win", [NJT, 128, 32, 128], F32, kind="ExternalInput").ap()
    cw_h = nc.dram_tensor("cw", [128, 80, 4], F32, kind="ExternalInput").ap()
    cb_h = nc.dram_tensor("cb", [128, 80], F32, kind="ExternalInput").ap()
    dtb_h = nc.dram_tensor("dtb", [128, 1], F32, kind="ExternalInput").ap()
    dvec_h = nc.dram_tensor("dvec", [1, 128], F32, kind="ExternalInput").ap()
    nw_h = nc.dram_tensor("nw", [128, 64], F32, kind="ExternalInput").ap()
    wout_h = nc.dram_tensor("wout", [IN, D], F32, kind="ExternalInput").ap()
    wk_h = nc.dram_tensor("wk", [D, D], F32, kind="ExternalInput").ap()
    fgate_h = nc.dram_tensor("fgate", [1, 16], F32, kind="ExternalInput").ap()
    fbias_h = nc.dram_tensor("fbias", [1, D], F32, kind="ExternalInput").ap()
    lA_h = nc.dram_tensor("lA", [16, 128, 1024], F32, kind="ExternalInput").ap()
    lB_h = nc.dram_tensor("lB", [16, 32, D], F32, kind="ExternalInput").ap()
    lAs_h = nc.dram_tensor("lAs", [128, 1024], F32, kind="ExternalInput").ap()
    lBs_h = nc.dram_tensor("lBs", [32, D], F32, kind="ExternalInput").ap()
    idf_h = nc.dram_tensor("idf", [128, 128], F32, kind="ExternalInput").ap()
    idb_h = nc.dram_tensor("idb", [128, 128], BF16, kind="ExternalInput").ap()
    mt_h = nc.dram_tensor("mt", [128, 128], F32, kind="ExternalInput").ap()
    acol_h = nc.dram_tensor("acol", [128, 1], F32, kind="ExternalInput").ap()
    ones_h = nc.dram_tensor("onesf", [1, 128], F32, kind="ExternalInput").ap()

    xbc_sp = nc.dram_tensor("xbc_sp", [CV, T], BF16, kind=kind_spill).ap()
    gate_sp = nc.dram_tensor("gate_sp", [IN, T], BF16, kind=kind_spill).ap()
    nrm_sp = nc.dram_tensor("nrm_sp", [T, IN], BF16, kind=kind_spill).ap()
    x1_sp = nc.dram_tensor("x1_sp", [T, D], F32, kind=kind_spill).ap()
    y_h = nc.dram_tensor("y", [T, D], F32, kind="ExternalOutput").ap()
    dbg = {}
    if debug:
        dbg["dtv"] = nc.dram_tensor("dbg_dtv", [128, T], F32, kind="ExternalOutput").ap()
        dbg["cs"] = nc.dram_tensor("dbg_cs", [128, T], F32, kind="ExternalOutput").ap()
        dbg["xnt"] = nc.dram_tensor("dbg_xnt", [128, 32 * 2048], BF16, kind="ExternalOutput").ap()
        dbg["stat"] = nc.dram_tensor("dbg_stat", [128, 64], F32, kind="ExternalOutput").ap()
        dbg["r16"] = nc.dram_tensor("dbg_r16", [128, 32], F32, kind="ExternalOutput").ap()
        dbg["axs"] = nc.dram_tensor("dbg_axs", [128, 2048], BF16, kind="ExternalOutput").ap()
        dbg["bs"] = nc.dram_tensor("dbg_bs", [128, 4096], BF16, kind="ExternalOutput").ap()
        dbg["xn2t"] = nc.dram_tensor("dbg_xn2t", [128, 32 * 2048], BF16, kind="ExternalOutput").ap()
        dbg["s1p"] = nc.dram_tensor("dbg_s1p", [128, 256], F32, kind="ExternalOutput").ap()

    prog = Prog()
    with ExitStack() as ctx:
        E = ctx.enter_context
        sems = {s: E(nc.semaphore(s)) for s in ALL_SEMS}
        sb = E(nc.sbuf_tensor("arena", [128, 200 * 1024], I8))
        psA = E(nc.psum_tensor("psA", [128, 2048], F32))
        psB = E(nc.psum_tensor("psB", [128, 2048], F32))
        ar = Arena(sb[:, :])

        idf = ar.alloc(128, F32)
        idb = ar.alloc(128, BF16)
        mt = ar.alloc(128, F32)
        acol = ar.alloc(1, F32)
        dtbc = ar.alloc(1, F32)
        onesf = ar.alloc(128, F32)
        dvrow = ar.alloc(128, F32)
        dbc = ar.alloc(128, F32)
        cwt = ar.alloc(80 * 4, F32)
        cbt = ar.alloc(80, F32)
        nwc = ar.alloc(64, F32)
        dtv = ar.alloc_high(T, F32)
        cs = ar.alloc_high(T, F32)
        zer128 = ar.alloc(128, F32)
        cst = ar.alloc(4, F32)

        m_persist = ar.mark()

        prog.dma("act", idf, idf_h, sem="dc0")
        prog.dma("act", idb, idb_h, sem="dc0")
        prog.dma("act", mt, mt_h, sem="dc0")
        prog.dma("act", acol, acol_h, sem="dc0")
        prog.dma("act", dtbc, dtb_h, sem="dc0")
        prog.dma("act", onesf[0:1, :], ones_h, sem="dc0")
        prog.dma("act", dvrow[0:1, :], dvec_h, sem="dc0")
        prog.dma("act", _r2(cwt, 80, 4), cw_h, sem="dc0")
        prog.dma("act", cbt, cb_h, sem="dc0")
        c0 = prog.dma("act", nwc, nw_h, sem="dc0")
        prog.op("dve", lambda v: v.memset(zer128, 0.0))
        prog.op("dve", lambda v: v.memset(cst[:, 0:1], EPS))
        prog.op("dve", lambda v: v.memset(cst[:, 1:2], 1.0))
        pb = psA[:, 0:128]
        m = prog.op("pe", lambda t: t.matmul(pb, onesf[0:1, :], dvrow[0:1, :], start=True, stop=True),
                    waits=[c0])
        mil_dbc = prog.op("dve", lambda v: v.tensor_copy(dbc, pb), waits=[m])

        m_p1 = ar.mark()
        xnT = ar.alloc(32 * 2048, BF16)
        xnT3 = _r2(xnT, 32, 2048)
        if debug:
            dbgstat = ar.alloc(64, F32)
        m_p1a = ar.mark()
        xt = [ar.alloc(D, F32), ar.alloc(D, F32)]
        xnb = ar.alloc(D, BF16)
        sqd = ar.alloc(2048, F32)
        s1 = ar.alloc(1, F32)
        s2a = ar.alloc(1, F32)
        s2b = ar.alloc(1, F32)
        meanc = ar.alloc(1, F32)
        msq = ar.alloc(1, F32)
        varc = ar.alloc(1, F32)
        rstd = ar.alloc(1, F32)
        nb = ar.alloc(1, F32)

        xload = [None, None]
        xn_done = [None, None]
        tr_done = None
        for tt in range(16):
            par = tt % 2
            xload[par] = prog.dma("sp", xt[par], x_h[tt * 128:(tt + 1) * 128, :],
                                  waits=[xn_done[par]] if xn_done[par] else [])
            wl = xload[par]
            m1 = prog.op("dve", lambda v, p=par: v.tensor_reduce(s1, xt[p], mybir.AxisListType.X, ALU.add),
                         waits=[wl], inc="dve")
            m2 = prog.op("act", lambda a, p=par: a.activation(sqd, xt[p][:, 0:2048], AF.Square, accum_out=s2a),
                         waits=[wl], inc="act")
            m3 = prog.op("act", lambda a, p=par: a.activation(sqd, xt[p][:, 2048:4096], AF.Square, accum_out=s2b),
                         inc="act")
            m4 = prog.op("dve", lambda v: v.tensor_scalar(meanc, s1, 1.0 / D, None, ALU.mult), waits=[m1], inc="dve")
            m5 = prog.op("dve", lambda v: v.tensor_tensor(msq, meanc, meanc, ALU.mult), waits=[m4], inc="dve")
            m6 = prog.op("dve", lambda v: v.tensor_tensor(varc, s2a, s2b, ALU.add), waits=[m3], inc="dve")
            m7 = prog.op("dve", lambda v: v.scalar_tensor_tensor(rstd, varc, 1.0 / D, msq, ALU.mult, ALU.subtract),
                         waits=[m5, m6], inc="dve")
            m8 = prog.op("act", lambda a: a.activation(rstd, rstd, AF.Ln, bias=cst[:, 0:1]), waits=[m7], inc="act")
            m9 = prog.op("act", lambda a: a.activation(rstd, rstd, AF.Exp, scale=-0.5), inc="act")
            m10 = prog.op("dve", lambda v: v.scalar_tensor_tensor(nb, meanc, -1.0, rstd, ALU.mult, ALU.mult),
                          waits=[m9], inc="dve")
            if debug:
                prog.op("dve", lambda v, tt=tt: v.tensor_copy(dbgstat[:, tt * 4 + 0:tt * 4 + 1], s1), inc="dve")
                prog.op("dve", lambda v, tt=tt: v.tensor_copy(dbgstat[:, tt * 4 + 1:tt * 4 + 2], varc), inc="dve")
                prog.op("dve", lambda v, tt=tt: v.tensor_copy(dbgstat[:, tt * 4 + 2:tt * 4 + 3], rstd), inc="dve")
                mstat = prog.op("dve", lambda v, tt=tt: v.tensor_copy(dbgstat[:, tt * 4 + 3:tt * 4 + 4], meanc), inc="dve")
            m11 = prog.op("dve", lambda v, p=par: v.tensor_scalar(xnb, xt[p], rstd, nb, ALU.mult, ALU.add),
                          waits=[m10] + ([tr_done] if tr_done else []), inc="dve")
            for grp in range(4):
                bank = psA[:, (grp % 2) * 512:(grp % 2) * 512 + 512].bitcast(BF16)
                bank3 = _r2(bank, 8, 128)
                last_pe = None
                for k in range(8):
                    kd = grp * 8 + k
                    w = [m11]
                    if tt == 0 and grp == 0:
                        w.append(mil_dbc)
                    if tt > 0 or grp > 1:
                        w.append(cp_done[grp % 2])
                    last_pe = prog.op("pe",
                                      lambda t, kd=kd, bank3=bank3, k=k: t.transpose(
                                          bank3[:, k, :], xnb[:, kd * 128:(kd + 1) * 128], idb),
                                      waits=w, inc="pe")
                if tt == 0 and grp == 0:
                    cp_done = [None, None]
                cp_done[grp % 2] = prog.op(
                    "dve",
                    lambda v, bank3=bank3, g=grp, tt=tt: v.tensor_copy(
                        xnT3[:, g * 8:(g + 1) * 8, tt * 128:(tt + 1) * 128], bank3),
                    waits=[last_pe], inc="dve")
            tr_done = cp_done[(3) % 2]
            xn_done[par] = m11 if False else cp_done[1]


        ar.reset(m_p1a)
        p1a_done = [prog.last("dve"), prog.last("act"), prog.last("sp"), prog.last("pe")]
        wbuf = [ar.alloc(4096, BF16), ar.alloc(4096, BF16)]
        ta = ar.alloc(2048, F32)
        tb = ar.alloc(2048, F32)
        convo = ar.alloc(2048, BF16)
        dtT_ap = tb

        psum = [psA[:, :], psB[:, :]]
        pe_after = {}
        cons_after = {}
        wdma = {}
        st_done = None
        cons_after[0] = cp_done[1]
        cons_after[1] = None

        for jt in range(NJT):
            par = jt % 2
            wdma[jt] = prog.dma("pool", _r2(wbuf[par], 32, 128), win_h[jt],
                                waits=[pe_after.get(jt - 2)] + (p1a_done if jt < 2 else []))
            wbuf3 = _r2(wbuf[par], 32, 128)
            last = None
            for ts in range(4):
                pbank = psum[par][:, ts * 512:(ts + 1) * 512]
                for kd in range(32):
                    w = []
                    if ts == 0 and kd == 0:
                        w = [wdma[jt], cons_after[par]] if cons_after[par] else [wdma[jt]]
                        if jt == 0:
                            w.append(cp_done[0])
                            w.append(cp_done[1])
                    last = prog.op("pe",
                                   lambda t, pbank=pbank, kd=kd, ts=ts, wb=wbuf3: t.matmul(
                                       pbank, wb[:, kd, :], xnT3[:, kd, ts * 512:(ts + 1) * 512],
                                       start=(kd == 0), stop=(kd == 31)),
                                   waits=w, inc="pe")
            pe_after[jt] = last

            if jt < 64:
                mact = prog.op("act",
                               lambda a, p=par: a.activation(convo, psum[p], AF.Silu),
                               waits=[last] + ([st_done] if st_done else []), inc="act")
                cons_after[par] = mact
                st_done = prog.dma("sp", gate_sp[jt * 128:(jt + 1) * 128, :], convo, waits=[mact])
            elif jt < 144:
                ct = jt - 64
                mw = [last]
                mtap = prog.op("dve",
                               lambda v, p=par, ct=ct: v.tensor_scalar(
                                   ta, psum[p], _r2(cwt, 80, 4)[:, ct, 3:4], None, ALU.mult),
                               waits=mw, inc="dve")
                for mm in range(1, 4):
                    mtap = prog.op("dve",
                                   lambda v, p=par, ct=ct, mm=mm: v.scalar_tensor_tensor(
                                       ta[:, mm:2048], psum[p][:, 0:2048 - mm],
                                       _r2(cwt, 80, 4)[:, ct, 3 - mm:4 - mm],
                                       ta[:, mm:2048], ALU.mult, ALU.add),
                                   inc="dve")
                cons_after[par] = mtap
                mact = prog.op("act",
                               lambda a, ct=ct: a.activation(convo, ta, AF.Silu, bias=cbt[:, ct:ct + 1]),
                               waits=[mtap] + ([st_done] if st_done else []), inc="act")
                st_done = prog.dma("sp", xbc_sp[ct * 128:(ct + 1) * 128, :], convo, waits=[mact])
            else:
                mcp = prog.op("dve", lambda v, p=par: v.tensor_copy(tb, psum[p]), waits=[last], inc="dve")
                cons_after[par] = mcp

        mdt = cons_after[144 % 2]
        ma = prog.op("act", lambda a: a.activation(ta, tb, AF.Abs, bias=dtbc), waits=[mdt], inc="act")
        me = prog.op("act", lambda a: a.activation(ta, ta, AF.Exp, scale=-1.0), waits=[ma], inc="act")
        ml = prog.op("act", lambda a: a.activation(ta, ta, AF.Ln, bias=cst[:, 1:2]), waits=[me], inc="act")
        mm_ = prog.op("dve", lambda v: v.tensor_scalar(tb, tb, dtbc, 0.0, ALU.add, ALU.max),
                      waits=[ml], inc="dve")
        msp = prog.op("dve", lambda v: v.tensor_tensor(tb, tb, ta, ALU.add), waits=[mm_], inc="dve")
        mcl = prog.op("dve", lambda v: v.tensor_scalar(dtv, tb, DT_MIN, DT_MAX, ALU.max, ALU.min),
                      waits=[msp], inc="dve")
        mda = prog.op("dve", lambda v: v.tensor_scalar(ta, dtv, acol, None, ALU.mult), waits=[mcl], inc="dve")
        mcs = None
        for c in range(NCH):
            mcs = prog.op("dve",
                          lambda v, c=c: v.tensor_tensor_scan(
                              cs[:, c * 128:(c + 1) * 128], ta[:, c * 128:(c + 1) * 128],
                              zer128, 0.0, ALU.add, ALU.add),
                          waits=[mda], inc="dve")
        if debug:
            prog.dma("sp", dbg["dtv"], dtv, waits=[mcl])
            prog.dma("sp", dbg["cs"], cs, waits=[mcs])
            prog.dma("sp", dbg["xnt"], xnT, waits=[mcs])
            prog.dma("sp", dbg["stat"], dbgstat, waits=[mstat])

        prog.barrier()
        ar.reset(m_p1)

        if upto == "p1":
            _finish(prog)
            _emit(nc, prog, sems)
            return nc

        m_p3 = ar.mark()
        emit_p3(nc, prog, ar, psA, psB, locals())
        prog.barrier()
        ar.reset(m_p3)
        ar.high = ar.nbytes
        if upto == "p3":
            _finish(prog)
            _emit(nc, prog, sems)
            return nc

        emit_p456(nc, prog, ar, psA, psB, locals())
        _finish(prog)
        _emit(nc, prog, sems)
    return nc


def _finish(prog):
    prog.barrier()


def _emit(nc, prog, sems):
    with nc.Block() as block:
        prog.emit(nc, block, sems)


def emit_p3(nc, prog, ar, psA, psB, env):
    ALU_ = ALU
    dtv, cs = env["dtv"], env["cs"]
    mt, idf, onesf, dbc, cst = env["mt"], env["idf"], env["onesf"], env["dbc"], env["cst"]
    xbc_sp, gate_sp, nrm_sp = env["xbc_sp"], env["gate_sp"], env["nrm_sp"]

    prev = ar.alloc(IN, F32)
    prevb = ar.alloc(IN, BF16)
    xnat = ar.alloc(IN, BF16)
    gateb = ar.alloc(IN, BF16)
    ynat = ar.alloc(IN, BF16)
    nrm = ar.alloc(IN, BF16)
    bt = ar.alloc(1024, BF16)
    ct_ = ar.alloc(1024, BF16)
    bnat = ar.alloc(1024, BF16)
    ssb = ar.alloc(1024, F32)
    t2m = [ar.alloc(1024, F32) for _ in range(3)]
    expb = [ar.alloc(1024, BF16) for _ in range(3)]
    scb = [ar.alloc(1024, BF16) for _ in range(3)]
    xdt8 = [ar.alloc(512, BF16) for _ in range(3)]
    xdd8 = [ar.alloc(512, BF16) for _ in range(3)]
    tsc = [ar.alloc(512, F32) for _ in range(3)]
    tsc2 = [ar.alloc(512, F32) for _ in range(3)]
    csTt = ar.alloc(128, F32)
    ncsT = ar.alloc(128, F32)
    ecsT = ar.alloc(128, F32)
    dtvT = ar.alloc(128, F32)
    ddwT = ar.alloc(128, F32)
    decay = ar.alloc(128, F32)
    ddw = ar.alloc(128, F32)
    cdc = ar.alloc(1, F32)
    cdrow = ar.alloc(128, F32)
    cdbc = ar.alloc(128, F32)
    sums = ar.alloc(8, F32)
    rstd8 = ar.alloc(8, F32)
    sqdump = ar.alloc(1024, F32)

    mil_dbc_env = None
    bt3 = _r2(bt, 8, 128)
    ct3 = _r2(ct_, 8, 128)
    bnat3 = _r2(bnat, 8, 128)
    prev3 = _r2(prev, 128, 64)

    T1 = [psA[:, 0:512], psA[:, 512:1024]]
    YD = [psA[:, 1024:1536], psB[:, 512:1024]]
    YO = [psA[:, 1536:2048], psB[:, 1024:1536]]
    ST = [psB[:, 0:512], psB[:, 1536:2048]]

    mz = prog.op("dve", lambda v: v.memset(prev, 0.0), inc="dve")

    fr = {"xnat": None, "gateb": None, "bt": None, "ct": None, "bnat": None,
          "nrm": None, "prevb_ready": None, "prev_scaled": None}
    grp = {}

    for c in range(NCH):
        c0, c1 = c * 128, (c + 1) * 128
        cssl = cs[:, c0:c1]
        dtsl = dtv[:, c0:c1]
        last = c == NCH - 1

        cb = [prog.last("pe"), prog.last("dve"), prog.last("act")]

        ldx = prog.dma("sp", xnat, xbc_sp[0:IN, c0:c1], waits=[fr["xnat"]] + cb[:0], transpose=True)
        ldg = prog.dma("sp", gateb, gate_sp[0:IN, c0:c1], waits=[fr["gateb"]], transpose=True)
        ldb = prog.dma("sp", bt3, xbc_sp[IN:IN + 1024, c0:c1].rearrange("(g n) l -> n g l", n=128),
                       waits=[fr["bt"]])
        ldc = prog.dma("sp", ct3, xbc_sp[IN + 1024:IN + 2048, c0:c1].rearrange("(g n) l -> n g l", n=128),
                       waits=[fr["ct"]])
        ldbn = prog.dma("sp", bnat, xbc_sp[IN:IN + 1024, c0:c1], waits=[fr["bnat"]], transpose=True)

        a0 = psA[:, 0:128]
        a1 = psA[:, 512:640]
        m1 = prog.op("pe", lambda t, a0=a0, cssl=cssl: t.transpose(a0, cssl, idf), waits=cb, inc="pe")
        m2 = prog.op("dve", lambda v, a0=a0: v.tensor_copy(csTt, a0), waits=[m1] + cb, inc="dve")
        m3 = prog.op("dve", lambda v: v.tensor_scalar(ncsT, csTt, -1.0, None, ALU_.mult), waits=[m2], inc="dve")
        me1 = prog.op("act", lambda a: a.activation(ecsT, csTt, AF.Exp), waits=[m2] + cb, inc="act")
        md1 = prog.op("act", lambda a, cssl=cssl, c1=c1: a.activation(
            decay, cssl, AF.Exp, scale=-1.0, bias=cs[:, c1 - 1:c1]), inc="act")
        mw1 = prog.op("dve", lambda v, dtsl=dtsl: v.tensor_tensor(ddw, dtsl, decay, ALU_.mult),
                      waits=[md1], inc="dve")
        m4 = prog.op("pe", lambda t, a1=a1, dtsl=dtsl: t.transpose(a1, dtsl, idf), waits=cb, inc="pe")
        m5 = prog.op("dve", lambda v, a1=a1: v.tensor_copy(dtvT, a1), waits=[m4], inc="dve")
        m6 = prog.op("pe", lambda t, a0=a0: t.transpose(a0, ddw, idf), waits=[m2, mw1], inc="pe")
        m7 = prog.op("dve", lambda v, a0=a0: v.tensor_copy(ddwT, a0), waits=[m6], inc="dve")
        prep_done = [m5, m7]
        if not last:
            mc1 = prog.op("act", lambda a, c1=c1: a.activation(cdc, cs[:, c1 - 1:c1], AF.Exp),
                          waits=cb, inc="act")
            mc2 = prog.op("pe", lambda t, a0=a0: t.transpose(a0[0:1, :], cdc, idf),
                          waits=[mc1, m7], inc="pe")
            mc3 = prog.op("dve", lambda v, a0=a0: v.tensor_copy(cdrow[0:1, :], a0[0:1, :]),
                          waits=[mc2], inc="dve")
            mc4 = prog.op("pe", lambda t, a0=a0: t.matmul(a0, onesf[0:1, :], cdrow[0:1, :],
                                                          start=True, stop=True),
                          waits=[mc3], inc="pe")
            mc5 = prog.op("dve", lambda v, a0=a0: v.tensor_copy(cdbc, a0), waits=[mc4], inc="dve")
            prep_done.append(mc5)
        sbanks = [psA[:, 1024:1536], psA[:, 1536:2048], psB[:, 512:1024], psB[:, 1024:1536]]
        mS = None
        for g in range(8):
            pbg = sbanks[g % 4][:, (g // 4) * 128:(g // 4 + 1) * 128]
            w = [ldb, ldc] + cb
            if g >= 4:
                w.append(grp.get(("ssb_ev", g - 4)))
            mm1 = prog.op("pe", lambda t, pbg=pbg, g=g: t.matmul(pbg, bt3[:, g, :], ct3[:, g, :],
                                                                start=True, stop=True),
                          waits=w, inc="pe")
            mm2 = prog.op("dve", lambda v, pbg=pbg, g=g: v.tensor_copy(ssb[:, g * 128:(g + 1) * 128], pbg),
                          waits=[mm1], inc="dve")
            grp[("ssb_ev", g)] = mm2
            mS = mm2
        if 0 < c < NCH - 1:
            cdb_b = _r2(cdbc, 128, 1)[:, :, 0:1].broadcast_to([128, 128, 64])
            mscale = prog.op("dve", lambda v, cdb_b=cdb_b: v.tensor_tensor(prev3, prev3, cdb_b, ALU_.mult),
                             waits=[mc5, fr["prevb_ready"]], inc="dve")
            fr["prev_scaled"] = mscale

        NGRP = 16
        for it in range(NGRP + 2):
            if it < NGRP:
                g8 = it
                p3_ = g8 % 3
                gbc = g8 // 2
                h0 = g8 * 8
                mt_b = mt.unsqueeze(1).broadcast_to([128, 8, 128])
                cs_b = csTt[:, h0:h0 + 8].unsqueeze(2).broadcast_to([128, 8, 128])
                mA = prog.op("dve", lambda v, p3_=p3_, mt_b=mt_b, cs_b=cs_b: v.tensor_tensor(
                    _r2(t2m[p3_], 8, 128), mt_b, cs_b, ALU_.add),
                    waits=[m2, m3, grp.get(("B", g8 - 3))])
                grp[("A", g8)] = mA
                wB = [mA]
                if g8 == 0:
                    wB += prep_done
                if grp.get(("C", g8 - 1)) is not None:
                    wB.append(grp[("C", g8 - 1)])
                mB = None
                for i in range(8):
                    t1slot = T1[i // 4][:, (i % 4) * 128:(i % 4 + 1) * 128]
                    mB = prog.op("pe", lambda t, i=i, p3_=p3_, t1slot=t1slot: t.transpose(
                        t1slot, t2m[p3_][:, i * 128:(i + 1) * 128], idf),
                        waits=wB if i == 0 else [])
                grp[("B", g8)] = mB
                wC = [mB, m3, grp.get(("D", g8 - 3))]
                mC = None
                for i in range(8):
                    h = g8 * 8 + i
                    t1slot = T1[i // 4][:, (i % 4) * 128:(i % 4 + 1) * 128]
                    mC = prog.op("act", lambda a, i=i, h=h, p3_=p3_, t1slot=t1slot: a.activation(
                        expb[p3_][:, i * 128:(i + 1) * 128], t1slot, AF.Exp, bias=ncsT[:, h:h + 1]),
                        waits=wC if i == 0 else [])
                grp[("C", g8)] = mC
                sb_b = _r2(ssb[:, gbc * 128:(gbc + 1) * 128], 1, 128).broadcast_to([128, 8, 128])
                mD = prog.op("dve", lambda v, p3_=p3_, sb_b=sb_b: v.tensor_tensor(
                    _r2(scb[p3_], 8, 128), _r2(expb[p3_], 8, 128), sb_b, ALU_.mult),
                    waits=[mC, mS, grp.get(("F", g8 - 3))])
                grp[("D", g8)] = mD
                wE = [ldx, m5, m7, grp.get(("F", g8 - 3))]
                xsl = _r2(xnat[:, h0 * 64:(h0 + 8) * 64], 8, 64)
                dtv_b = dtvT[:, h0:h0 + 8].unsqueeze(2).broadcast_to([128, 8, 64])
                mE = prog.op("pool", lambda v, p3_=p3_, xsl=xsl, dtv_b=dtv_b: v.tensor_tensor(
                    _r2(xdt8[p3_], 8, 64), xsl, dtv_b, ALU_.mult), waits=wE)
                if not last:
                    ddw_b = ddwT[:, h0:h0 + 8].unsqueeze(2).broadcast_to([128, 8, 64])
                    mE = prog.op("pool", lambda v, p3_=p3_, xsl=xsl, ddw_b=ddw_b: v.tensor_tensor(
                        _r2(xdd8[p3_], 8, 64), xsl, ddw_b, ALU_.mult), waits=[mE])
                grp[("E", g8)] = mE
                d_b = dbc[:, h0:h0 + 8].unsqueeze(2).broadcast_to([128, 8, 64])
                grp[("g0", g8)] = prog.op("pool", lambda v, xsl=xsl, d_b=d_b, p3_=p3_: v.tensor_tensor(
                    _r2(tsc[p3_], 8, 64), xsl, d_b, ALU_.mult),
                    waits=[ldx, grp.get(("G", g8 - 3)), mil_dbc_env])
            if it >= 2:
                g8 = it - 2
                par = g8 % 2
                p3_ = g8 % 3
                gbc = g8 // 2
                h0 = g8 * 8
                ysl = _r2(ynat[:, h0 * 64:(h0 + 8) * 64], 8, 64)
                wF = [grp[("D", g8)], grp[("E", g8)], ldc, ldbn]
                if c > 0:
                    wF.append(fr["prevb_ready"])
                if grp.get(("H", g8 - 2)) is not None:
                    wF.append(grp[("H", g8 - 2)])
                if grp.get(("G", g8 - 2)) is not None:
                    wF.append(grp[("G", g8 - 2)])
                mF = None
                for i in range(8):
                    h = g8 * 8 + i
                    first = (i == 0)
                    mF = prog.op("pe", lambda t, i=i, par=par, p3_=p3_: t.matmul(
                        YD[par][:, i * 64:(i + 1) * 64], scb[p3_][:, i * 128:(i + 1) * 128],
                        xdt8[p3_][:, i * 64:(i + 1) * 64], start=True, stop=True),
                        waits=wF if first else [])
                    if c > 0:
                        mF = prog.op("pe", lambda t, i=i, h=h, par=par, gbc=gbc: t.matmul(
                            YO[par][:, i * 64:(i + 1) * 64], ct3[:, gbc, :],
                            prevb[:, h * 64:(h + 1) * 64], start=True, stop=True))
                    if not last:
                        mF = prog.op("pe", lambda t, i=i, par=par, p3_=p3_, gbc=gbc: t.matmul(
                            ST[par][:, i * 64:(i + 1) * 64], bnat3[:, gbc, :],
                            xdd8[p3_][:, i * 64:(i + 1) * 64], start=True, stop=True))
                grp[("F", g8)] = mF
                mg0 = grp[("g0", g8)]
                if c == 0:
                    mG = prog.op("dve", lambda v, ysl=ysl, p3_=p3_, par=par: v.tensor_tensor(
                        ysl, _r2(tsc[p3_], 8, 64), _r2(YD[par], 8, 64), ALU_.add),
                        waits=[mF, mg0])
                else:
                    ecs_b = ecsT[:, h0:h0 + 8].unsqueeze(2).broadcast_to([128, 8, 64])
                    mg1 = prog.op("dve", lambda v, par=par, p3_=p3_, ecs_b=ecs_b: v.tensor_tensor(
                        _r2(tsc2[p3_], 8, 64), _r2(YO[par], 8, 64), ecs_b, ALU_.mult),
                        waits=[mF, me1, grp.get(("G", g8 - 3))])
                    mg2 = prog.op("dve", lambda v, par=par, p3_=p3_: v.tensor_tensor(
                        _r2(tsc[p3_], 8, 64), _r2(tsc[p3_], 8, 64), _r2(YD[par], 8, 64), ALU_.add),
                        waits=[mg0, mF])
                    mG = prog.op("dve", lambda v, ysl=ysl, p3_=p3_: v.tensor_tensor(
                        ysl, _r2(tsc[p3_], 8, 64), _r2(tsc2[p3_], 8, 64), ALU_.add),
                        waits=[mg1, mg2])
                grp[("G", g8)] = mG
                if not last:
                    wH = [mF]
                    if c == 0:
                        wH.append(mz)
                    elif fr["prev_scaled"] is not None:
                        wH.append(fr["prev_scaled"])
                    mH = prog.op("dve", lambda v, g8=g8, par=par: v.tensor_tensor(
                        prev[:, g8 * 512:(g8 + 1) * 512], prev[:, g8 * 512:(g8 + 1) * 512],
                        ST[par], ALU_.add), waits=wH)
                else:
                    mH = None
                grp[("H", g8)] = mH

        lastG = grp[("G", NGRP - 1)]
        lastH = grp[("H", NGRP - 1)]
        if not last:
            fr["prevb_ready"] = prog.op("dve", lambda v: v.tensor_copy(prevb, prev),
                                        waits=[lastH], inc="dve")
        myg = prog.op("dve", lambda v: v.tensor_tensor(ynat, ynat, gateb, ALU_.mult),
                      waits=[lastG, ldg], inc="dve")
        fr["gateb"] = myg
        fr["xnat"] = lastG
        fr["bt"] = mS
        fr["ct"] = grp[("F", NGRP - 1)]
        fr["bnat"] = grp[("F", NGRP - 1)]
        msq_ = None
        for g in range(8):
            msq_ = prog.op("act", lambda a, g=g: a.activation(
                sqdump, ynat[:, g * 1024:(g + 1) * 1024], AF.Square, accum_out=sums[:, g:g + 1]),
                waits=[myg] if g == 0 else [], inc="act")
        mln = prog.op("act", lambda a: a.activation(rstd8, sums, AF.Ln, scale=1.0 / 1024.0,
                                                    bias=cst[:, 0:1]), waits=[msq_], inc="act")
        mex = prog.op("act", lambda a: a.activation(rstd8, rstd8, AF.Exp, scale=-0.5),
                      waits=[mln], inc="act")
        mn = None
        for g in range(8):
            mn = prog.op("dve", lambda v, g=g: v.tensor_scalar(
                nrm[:, g * 1024:(g + 1) * 1024], ynat[:, g * 1024:(g + 1) * 1024],
                rstd8[:, g:g + 1], None, ALU_.mult),
                waits=([mex, fr["nrm"]] if g == 0 else []), inc="dve")
        fr["nrm"] = prog.dma("sp", nrm_sp[c0:c1, :], nrm, waits=[mn])
        grp = {}


def emit_p456(nc, prog, ar, psA, psB, env):
    nwc, idb, onesf, cst = env["nwc"], env["idb"], env["onesf"], env["cst"]
    nrm_sp, x1_sp, y_h, x_h = env["nrm_sp"], env["x1_sp"], env["y_h"], env["x_h"]
    wout_h, wk_h = env["wout_h"], env["wk_h"]
    fgate_h, fbias_h = env["fgate_h"], env["fbias_h"]
    lA_h, lB_h, lAs_h, lBs_h = env["lA_h"], env["lB_h"], env["lAs_h"], env["lBs_h"]

    rstd16 = ar.alloc(16, F32)
    nb16 = ar.alloc(16, F32)
    m_p4 = ar.mark()
    normT = ar.alloc(64 * 1024, BF16)
    normT3 = _r2(normT, 64, 1024)
    wbuf = [ar.alloc(1024, BF16) for _ in range(6)]
    xres = [ar.alloc(512, F32) for _ in range(8)]
    x1b = [ar.alloc(512, F32), ar.alloc(512, F32)]
    dmp = ar.alloc(512, F32)
    s1p = ar.alloc(128, F32)
    s2p = ar.alloc(128, F32)
    sc1 = ar.alloc(1, F32)
    sc2 = ar.alloc(1, F32)
    sc3 = ar.alloc(1, F32)

    pbank = [psA[:, i * 512:(i + 1) * 512] for i in range(4)] + \
            [psB[:, i * 512:(i + 1) * 512] for i in range(4)]

    p3_done = [prog.last("pe"), prog.last("dve"), prog.last("act"), prog.last("sp"), prog.last("pool")]
    bank_cons = [None] * 8
    evac_hist = {}
    mstat = None
    wd = {}
    pe_ct = {}
    x1b_st = [None, None]
    for half in range(2):
        mnt = []
        for ct in range(64):
            ld = prog.dma("sp", normT3[:, ct, :],
                          nrm_sp[half * 1024:(half + 1) * 1024, ct * 128:(ct + 1) * 128],
                          waits=(p3_done if (half == 0 and ct == 0) else []) +
                                ([evac_hist.get(("half", half - 1))] if ct == 0 else []),
                          transpose=True)
            mnt.append(prog.op("dve", lambda v, ct=ct: v.tensor_scalar(
                normT3[:, ct, :], normT3[:, ct, :], nwc[:, ct:ct + 1], None, ALU.mult),
                waits=[ld]))
        for js in range(8):
            jg = half * 8 + js
            mlast = None
            for tt in range(8):
                tg = half * 8 + tt
                evac_hist[("xl", jg, tt)] = prog.dma(
                    "sp", xres[tt], x_h[tg * 128:(tg + 1) * 128, js * 512:(js + 1) * 512],
                    waits=[evac_hist.get(("xres", half, js - 1, tt))])
            for cp in range(32):
                par = (jg * 32 + cp) % 6
                wd[jg, cp] = prog.dma(
                    "pool", _r2(wbuf[par], 2, 512),
                    wout_h[cp * 256:(cp + 1) * 256, js * 512:(js + 1) * 512]
                    .rearrange("(b p) j -> p b j", p=128),
                    waits=[wd.get((jg, cp - 6), wd.get((jg - 1, 26 + cp))),
                           pe_ct.get((jg, cp - 6), pe_ct.get((jg - 1, 26 + cp)))] +
                          (p3_done if (half == 0 and js == 0 and cp < 6) else []))
                for b in range(2):
                    ct = cp * 2 + b
                    for tt in range(8):
                        w = [wd[jg, cp], mnt[ct]] if tt == 0 else []
                        if ct == 0:
                            w = [wd[jg, cp], mnt[63], bank_cons[tt]]
                        mlast = prog.op("pe", lambda t, ct=ct, tt=tt, par=par, b=b: t.matmul(
                            pbank[tt], normT3[:, ct, tt * 128:(tt + 1) * 128],
                            _r2(wbuf[par], 2, 512)[:, b, :],
                            start=(ct == 0), stop=(ct == 63)),
                            waits=[x for x in w if x is not None])
                pe_ct[jg, cp] = mlast
            for tt in range(8):
                tg = half * 8 + tt
                xl = evac_hist[("xl", jg, tt)]
                m1 = prog.op("dve", lambda v, tt=tt: v.tensor_tensor(
                    x1b[tt % 2], pbank[tt], xres[tt], ALU.add),
                    waits=[mlast, xl, x1b_st[tt % 2]])
                bank_cons[tt] = m1
                evac_hist[("xres", half, js, tt)] = m1
                col = tg * 8 + js
                m2 = prog.op("act", lambda a, tt=tt, col=col: a.activation(
                    dmp, x1b[tt % 2], AF.Copy, accum_out=s1p[:, col:col + 1]), waits=[m1])
                m3 = prog.op("act", lambda a, tt=tt, col=col: a.activation(
                    dmp, x1b[tt % 2], AF.Square, accum_out=s2p[:, col:col + 1]), waits=[m2])
                mstat = m3
                x1b_st[tt % 2] = prog.dma("sp", x1_sp[tg * 128:(tg + 1) * 128, js * 512:(js + 1) * 512],
                                          x1b[tt % 2], waits=[m3])
        evac_hist[("half", half)] = mstat

    mr = None
    for tg in range(16):
        w = [mstat] if tg == 0 else [mr]
        a1 = prog.op("dve", lambda v, tg=tg: v.tensor_reduce(
            sc1, s1p[:, tg * 8:(tg + 1) * 8], mybir.AxisListType.X, ALU.add), waits=w)
        a2 = prog.op("dve", lambda v, tg=tg: v.tensor_reduce(
            sc2, s2p[:, tg * 8:(tg + 1) * 8], mybir.AxisListType.X, ALU.add), waits=[a1])
        a3 = prog.op("dve", lambda v: v.tensor_scalar(sc1, sc1, 1.0 / D, None, ALU.mult), waits=[a2])
        a4 = prog.op("dve", lambda v: v.tensor_tensor(sc3, sc1, sc1, ALU.mult), waits=[a3])
        a5 = prog.op("dve", lambda v, tg=tg: v.scalar_tensor_tensor(
            rstd16[:, tg:tg + 1], sc2, 1.0 / D, sc3, ALU.mult, ALU.subtract), waits=[a4])
        a6 = prog.op("act", lambda a, tg=tg: a.activation(
            rstd16[:, tg:tg + 1], rstd16[:, tg:tg + 1], AF.Ln, bias=cst[:, 0:1]), waits=[a5])
        a7 = prog.op("act", lambda a, tg=tg: a.activation(
            rstd16[:, tg:tg + 1], rstd16[:, tg:tg + 1], AF.Exp, scale=-0.5), waits=[a6])
        mr = prog.op("dve", lambda v, tg=tg: v.scalar_tensor_tensor(
            nb16[:, tg:tg + 1], sc1, -1.0, rstd16[:, tg:tg + 1], ALU.mult, ALU.mult), waits=[a7])

    prog.barrier()
    ar.reset(m_p4)

    xn2T = ar.alloc(32 * 2048, BF16)
    xn2T3 = _r2(xn2T, 32, 2048)
    AT = ar.alloc(32 * 64, BF16)
    AT3 = _r2(AT, 32, 64)
    BS = ar.alloc(4096, BF16)
    AxS = ar.alloc(2048, BF16)
    gbc = ar.alloc(16, F32)
    m_p5s = ar.mark()
    x1t = [ar.alloc(D, F32), ar.alloc(D, F32)]
    xn2b = ar.alloc(D, BF16)

    tr_done = None
    m1hist = {}
    for tg in range(16):
        mload = prog.dma("sp", x1t[tg % 2], x1_sp[tg * 128:(tg + 1) * 128, :],
                         waits=[m1hist.get(tg - 2)])
        m1 = prog.op("dve", lambda v, tg=tg: v.tensor_scalar(
            xn2b, x1t[tg % 2], rstd16[:, tg:tg + 1], nb16[:, tg:tg + 1], ALU.mult, ALU.add),
            waits=[mload, tr_done])
        m1hist[tg] = m1
        cp_done = [None, None]
        for grp_ in range(4):
            bank = psA[:, (grp_ % 2) * 512:(grp_ % 2) * 512 + 512].bitcast(BF16)
            bank3 = _r2(bank, 8, 128)
            lastpe = None
            for k in range(8):
                kd = grp_ * 8 + k
                w = [m1]
                if grp_ > 1:
                    w.append(cp_done[grp_ % 2])
                elif tg == 0 and grp_ < 2:
                    w.append(p3_done[0])
                lastpe = prog.op("pe", lambda t, kd=kd, bank3=bank3, k=k: t.transpose(
                    bank3[:, k, :], xn2b[:, kd * 128:(kd + 1) * 128], idb),
                    waits=w if k == 0 else [], inc="pe")
            cp_done[grp_ % 2] = prog.op("dve", lambda v, bank3=bank3, g=grp_, tg=tg: v.tensor_copy(
                xn2T3[:, g * 8:(g + 1) * 8, tg * 128:(tg + 1) * 128], bank3), waits=[lastpe])
        tr_done = cp_done[1]

    ar.reset(m_p5s)
    aacc = ar.alloc(1024, F32)
    lbufA = ar.alloc(1024, F32)
    lbufB = ar.alloc(4096, F32)
    bacc = ar.alloc(4096, F32)

    ldg = prog.dma("act", gbc[0:1, 0:16], fgate_h, sem="dc5")
    pg = psB[:, 0:16]
    mg1 = prog.op("pe", lambda t: t.matmul(pg, onesf[0:1, :], gbc[0:1, 0:16], start=True, stop=True),
                  waits=[ldg, tr_done])
    mg2 = prog.op("dve", lambda v: v.tensor_copy(gbc, pg), waits=[mg1])
    macc = None
    for n in range(16):
        lda = prog.dma("sp", _r2(lbufA, 32, 32), lA_h[n],
                       waits=[macc if n > 0 else None] + ([tr_done, mload] if n == 0 else []))
        if n == 0:
            macc = prog.op("dve", lambda v: v.tensor_scalar(
                aacc, lbufA, gbc[:, 0:1], None, ALU.mult), waits=[lda, mg2])
        else:
            macc = prog.op("dve", lambda v, n=n: v.scalar_tensor_tensor(
                aacc, lbufA, gbc[:, n:n + 1], aacc, ALU.mult, ALU.add),
                waits=[lda, macc])
    mat1 = prog.op("dve", lambda v: v.tensor_copy(AT3[:, :, 0:32], _r2(aacc, 32, 32)), waits=[macc])
    ldas = prog.dma("sp", _r2(lbufA, 32, 32), lAs_h, waits=[macc])
    mat2 = prog.op("dve", lambda v: v.tensor_copy(AT3[:, :, 32:64], _r2(lbufA, 32, 32)),
                   waits=[ldas, mat1])
    mbcc = None
    for n in range(16):
        ldb_ = prog.dma("pool", lbufB[0:32, :], lB_h[n],
                        waits=[mbcc if n > 0 else None] + ([tr_done, mload] if n == 0 else []))
        if n == 0:
            mbcc = prog.op("dve", lambda v: v.tensor_scalar(
                bacc[0:32, :], lbufB[0:32, :], gbc[0:32, 0:1], None, ALU.mult),
                waits=[ldb_, mg2])
        else:
            mbcc = prog.op("dve", lambda v, n=n: v.scalar_tensor_tensor(
                bacc[0:32, :], lbufB[0:32, :], gbc[0:32, n:n + 1], bacc[0:32, :],
                ALU.mult, ALU.add), waits=[ldb_, mbcc])
    mbs1 = prog.op("dve", lambda v: v.tensor_copy(BS[0:32, :], bacc[0:32, :]), waits=[mbcc])
    ldbs = prog.dma("pool", BS[32:64, :], lBs_h, sem="dc6", waits=[mbs1])
    ldfb = prog.dma("pool", BS[64:65, :], fbias_h, sem="dc6", waits=[mbs1])
    mones = prog.op("dve", lambda v: v.memset(AxS[64:65, :], 1.0))
    axd = None
    for strip in range(4):
        pax = psB[:, 512 + (strip % 2) * 512: 512 + (strip % 2) * 512 + 512]
        ml = None
        for kd in range(32):
            w = [mat2, tr_done] if kd == 0 else []
            if kd == 0 and axd is not None and strip >= 2:
                w.append(axd)
            ml = prog.op("pe", lambda t, kd=kd, strip=strip, pax=pax: t.matmul(
                pax[0:64, :], AT3[:, kd, :], xn2T3[:, kd, strip * 512:(strip + 1) * 512],
                start=(kd == 0), stop=(kd == 31)), waits=w)
        axd = prog.op("dve", lambda v, strip=strip, pax=pax: v.tensor_copy(
            AxS[0:64, strip * 512:(strip + 1) * 512], pax[0:64, :]), waits=[ml])

    dbgd = env.get("dbg", {})
    if dbgd:
        prog.dma("sp", dbgd["r16"][:, 0:16], rstd16, waits=[mr])
        prog.dma("sp", dbgd["r16"][:, 16:32], nb16, waits=[mr])
        prog.dma("sp", dbgd["axs"], AxS, waits=[axd, mones])
        prog.dma("sp", dbgd["bs"], BS, waits=[mbs1, ldbs, ldfb])
        prog.dma("sp", dbgd["xn2t"], xn2T, waits=[tr_done])
    ar.reset(m_p5s)
    kb = [ar.alloc(1024, BF16) for _ in range(6)]
    x1r = [ar.alloc(512, F32) for _ in range(8)]
    ob = [ar.alloc(512, F32), ar.alloc(512, F32)]
    bank_cons = [None] * 8
    kd_ = {}
    pe_ct = {}
    mlast = None
    ecnt = 0
    for half in range(2):
        for js in range(8):
            ji = half * 8 + js
            for tt in range(8):
                tg = half * 8 + tt
                evac_hist[("p6xl", ji, tt)] = prog.dma(
                    "sp", x1r[tt], x1_sp[tg * 128:(tg + 1) * 128, js * 512:(js + 1) * 512],
                    waits=[evac_hist.get(("p6x", ji - 1, tt))] +
                          ([mat2, axd, mbs1] if ji == 0 else []))
            for cp in range(16):
                par = (ji * 16 + cp) % 6
                kd_[ji, cp] = prog.dma(
                    "pool", _r2(kb[par], 2, 512),
                    wk_h[cp * 256:(cp + 1) * 256, js * 512:(js + 1) * 512]
                    .rearrange("(b p) j -> p b j", p=128),
                    waits=[kd_.get((ji, cp - 6), kd_.get((ji - 1, 10 + cp))),
                           pe_ct.get((ji, cp - 6), pe_ct.get((ji - 1, 10 + cp))),
                           ldbs if (ji == 0 and cp < 6) else None])
                for b in range(2):
                    ct = cp * 2 + b
                    for tt in range(8):
                        tg = half * 8 + tt
                        w = []
                        if tt == 0:
                            w = [kd_[ji, cp]]
                        if ct == 0:
                            w = [kd_[ji, cp], bank_cons[tt], tr_done]
                            if ji == 0:
                                w += [mg2, axd]
                        mlast = prog.op("pe", lambda t, ct=ct, tt=tt, tg=tg, par=par, b=b: t.matmul(
                            pbank[tt], xn2T3[:, ct, tg * 128:(tg + 1) * 128],
                            _r2(kb[par], 2, 512)[:, b, :],
                            start=(ct == 0), stop=False), waits=[x for x in w if x is not None])
                pe_ct[ji, cp] = mlast
            for tt in range(8):
                tg = half * 8 + tt
                w = [axd, mones, ldbs, ldfb] if (ji == 0 and tt == 0) else []
                mlast = prog.op("pe", lambda t, tt=tt, tg=tg, js=js: t.matmul(
                    pbank[tt], AxS[0:65, tg * 128:(tg + 1) * 128], BS[0:65, js * 512:(js + 1) * 512],
                    start=False, stop=True), waits=w)
            mmdone = mlast
            for tt in range(8):
                tg = half * 8 + tt
                par = ecnt % 2
                m1 = prog.op("dve", lambda v, tt=tt, par=par: v.tensor_tensor(
                    ob[par], pbank[tt], x1r[tt], ALU.add),
                    waits=[mmdone, evac_hist[("p6xl", ji, tt)], evac_hist.get(("p6s", ecnt - 2))])
                bank_cons[tt] = m1
                evac_hist[("p6x", ji, tt)] = m1
                evac_hist[("p6s", ecnt)] = prog.dma(
                    "sp", y_h[tg * 128:(tg + 1) * 128, js * 512:(js + 1) * 512], ob[par], waits=[m1])
                ecnt += 1



def _prep_core_inputs(inputs, b):
    f32 = np.float32
    win = inputs["L0_attn_in_proj_w"]
    d = {
        "x": np.ascontiguousarray(inputs["L0_attn_hidden_states"][b]).astype(f32, copy=False),
        "win": np.ascontiguousarray(win.reshape(NJT, 128, 32, 128).transpose(0, 3, 2, 1)),
        "cw": np.ascontiguousarray(inputs["L0_attn_conv_w"].reshape(80, 128, 4).transpose(1, 0, 2)),
        "cb": np.ascontiguousarray(inputs["L0_attn_conv_b"].reshape(80, 128).T),
        "dtb": np.ascontiguousarray(inputs["L0_attn_dt_bias"].reshape(128, 1)),
        "dvec": np.ascontiguousarray(inputs["L0_attn_D"].reshape(1, 128)),
        "nw": np.ascontiguousarray(inputs["L0_attn_norm_w"].reshape(64, 128).T),
        "wout": np.ascontiguousarray(inputs["L0_attn_out_proj_w"].T),
        "wk": np.ascontiguousarray(inputs["L0_ffn_kernel"]),
        "fgate": np.ascontiguousarray(inputs["L0_ffn_gate"][b].reshape(1, 16)),
        "fbias": np.ascontiguousarray(inputs["L0_ffn_bias"].reshape(1, D)),
        "lA": np.ascontiguousarray(inputs["L0_ffn_lora_A"].transpose(0, 2, 1).reshape(16, 32, 128, 32)
                                   .transpose(0, 2, 1, 3).reshape(16, 128, 1024)),
        "lB": np.ascontiguousarray(inputs["L0_ffn_lora_B"].transpose(0, 2, 1)),
        "lAs": np.ascontiguousarray(inputs["L0_ffn_lora_A_shared"].T.reshape(32, 128, 32)
                                    .transpose(1, 0, 2).reshape(128, 1024)),
        "lBs": np.ascontiguousarray(inputs["L0_ffn_lora_B_shared"].T),
        "idf": np.eye(128, dtype=f32),
        "idb": np.eye(128, dtype=BFNP),
        "mt": np.where(np.arange(128)[None, :] <= np.arange(128)[:, None], 0.0, NEG).astype(f32),
        "acol": -(np.arange(1, 129, dtype=f32)).reshape(128, 1),
        "onesf": np.ones((1, 128), dtype=f32),
    }
    return d


def kernel(**inputs):
    nc = build(upto="full", debug=False)
    in_maps = [_prep_core_inputs(inputs, b) for b in range(8)]
    res = run_bass_kernel_spmd(nc, in_maps, core_ids=list(range(8)))
    out = np.stack([np.asarray(res.results[b]["y"]) for b in range(8)], axis=0)
    return out.astype(np.float32, copy=False)
